# version 22
# baseline (speedup 1.0000x reference)
import numpy as np
from contextlib import ExitStack

import concourse.bass as bass
import concourse.mybir as mybir
from concourse.bass_utils import run_bass_kernel_spmd

F32 = mybir.dt.float32
BF16 = mybir.dt.bfloat16
U32 = mybir.dt.uint32
AF = mybir.ActivationFunctionType
ALU = mybir.AluOpType
AX = mybir.AxisListType
IOoA = bass.IndirectOffsetOnAxis

NCORES = 8
C = 1024
NCH = 8
H = 16
D = 64
E = 8
DFF = 4096
NFC = DFF // 128
EPS = 1e-5
TF = 512
ST = 512
BIG = 8_000_000.0

PP_BK = 0
PP_BO = 8
PP_LNG = 16
PP_LNB = 24
PP_WG = 32
PP_B1 = 96
PP_B2 = 128
PP_PREF = 136
PP_CBASE = 137
PP_COLS = 144


class Sem:
    def __init__(self, ctx, nc, name):
        self.h = ctx.enter_context(nc.semaphore(name))
        self.n = 0

    def inc(self, instr, k=1):
        instr.then_inc(self.h, k)
        self.n += k
        return instr

    def incd(self, instr):
        return self.inc(instr, 16)

    def wait(self, eng, val=None):
        eng.wait_ge(self.h, self.n if val is None else val)


def build_program(N=4096, CAP=5120):
    NT = N // TF
    NSUB = N // 128
    NST = CAP // ST
    CAPB = CAP // 128
    CHK_T = 1024 if N % 1024 == 0 else N
    NCHK_T = N // CHK_T
    CHK_E = 1024 if CAP % 1024 == 0 else CAP
    NCHK_E = CAP // CHK_E

    nc = bass.Bass("TRN2", target_bir_lowering=False, debug=False)

    xT = nc.dram_tensor("xT", [C, N], F32, kind="ExternalInput").ap()
    wqT = nc.dram_tensor("wqT", [C, C], F32, kind="ExternalInput").ap()
    wkT = nc.dram_tensor("wkT", [C, C], F32, kind="ExternalInput").ap()
    wvT = nc.dram_tensor("wvT", [C, C], F32, kind="ExternalInput").ap()
    woT = nc.dram_tensor("woT", [C, C], F32, kind="ExternalInput").ap()
    w1e = nc.dram_tensor("w1e", [C, DFF], F32, kind="ExternalInput").ap()
    w2e = nc.dram_tensor("w2e", [DFF, C], F32, kind="ExternalInput").ap()
    pp_d = nc.dram_tensor("pp_d", [128, PP_COLS], F32, kind="ExternalInput").ap()
    rows_d = nc.dram_tensor("rows_d", [3, C], F32, kind="ExternalInput").ap()
    consts_d = nc.dram_tensor("consts_d", [128, 256], F32, kind="ExternalInput").ap()

    out_d = nc.dram_tensor("out", [N, C], F32, kind="ExternalOutput").ap()

    x2_dram = nc.dram_tensor("x2_dram", [N, C], BF16).ap()
    ln2_cin = nc.dram_tensor("ln2_cin", [N, C], BF16).ap()
    tok_all = nc.dram_tensor("tok_all", [NCORES * N, C], BF16).ap()
    cnt_cin = nc.dram_tensor("cnt_cin", [1, E], F32).ap()
    cnt_all = nc.dram_tensor("cnt_all", [NCORES, E], F32).ap()
    src_cin = nc.dram_tensor("src_cin", [E * CAP, 1], F32).ap()
    src_all = nc.dram_tensor("src_all", [E * CAP, 1], F32).ap()
    cslotu_d = nc.dram_tensor("cslotu", [CAPB, 1], U32, kind="ExternalInput").ap()
    gidx_d = nc.dram_tensor("gidx_d", [128, N // 128], F32, kind="ExternalInput").ap()
    eout_cin = nc.dram_tensor("eout_cin", [CAP, C], BF16).ap()
    eout_all = nc.dram_tensor("eout_all", [E * CAP, C], BF16).ap()
    basec_d = nc.dram_tensor("basec_d", [1, (N // 128) * E], F32).ap()

    RG = [list(range(NCORES))]

    with ExitStack() as ctx:
        sbg = lambda name, shape, dt: ctx.enter_context(nc.sbuf_tensor(name, shape, dt))

        pp = sbg("pp", [128, PP_COLS], F32)
        consts = sbg("consts", [128, 256], F32)
        idn_bf = sbg("idn_bf", [128, 128], BF16)
        ones_col_bf = sbg("ones_col_bf", [128, 1], BF16)
        ones_row_bf = sbg("ones_row_bf", [1, 128], BF16)
        ones_t = sbg("ones_t", [65, 128], F32)
        ones_row_f = ones_t[0:1, :]
        ones_row_f64 = ones_t[64:65, :]
        ones_col_f = sbg("ones_col_f", [128, 1], F32)
        one_f = sbg("one_f", [1, 1], F32)
        eps_t = sbg("eps_t", [65, 1], F32)
        eps1 = eps_t[0:1, :]
        eps1_64 = eps_t[64:65, :]
        bq_row = sbg("bq_row", [1, C], BF16)
        bv_row = sbg("bv_row", [1, C], BF16)
        zero_f = sbg("zero_f", [128, 128], F32)
        zrow_bf = sbg("zrow_bf", [1, 512], BF16)
        iota8_row = sbg("iota8_row", [128, E], F32)
        gidx_all = sbg("gidx_all", [128, NSUB], F32)
        glog = sbg("glog", [128, NSUB * E], F32)
        grec = sbg("grec", [128, NSUB], F32)
        eidx = sbg("eidx", [128, NSUB], F32)
        oh_s = sbg("oh_s", [128, NSUB * E], F32)
        lpos_blk = sbg("lpos_blk", [128, NSUB * E], F32)
        base_bc = sbg("base_bc", [128, NSUB * E], F32)
        pos_s = sbg("pos_s", [128, NSUB], F32)
        gp_s = sbg("gp_s", [128, NSUB], F32)
        dst_u = sbg("dst_u", [128, NSUB], U32)
        dst2_u = sbg("dst2_u", [128, NSUB], U32)
        chk_u = sbg("chk_u", [128, NSUB], U32)
        scr = sbg("scr", [128, NSUB * E], F32)
        carr = sbg("carr", [8, NSUB + 1], F32)
        basec = sbg("basec", [8, NSUB], F32)
        base_row = sbg("base_row", [1, NSUB * E], F32)
        cnt_sb = sbg("cnt_sb", [8, E], F32)
        orow8 = sbg("orow8", [1, E], F32)

        p12 = ExitStack()
        sb12 = lambda name, shape, dt: p12.enter_context(nc.sbuf_tensor(name, shape, dt))
        strow_t = sb12("strow_t", [65, 3 * TF], F32)
        strow = strow_t[0:1, :]
        strow2 = strow_t[64:65, :]
        sqbf = sb12("sqbf", [128, NCH * TF], BF16)
        cbf = sb12("cbf", [128, NCH * TF], BF16)
        qs_fm = sb12("qs_fm", [128, NCH * N], BF16)
        s_bf = sb12("s_bf", [128, H * D // 2], BF16)
        z_sb = sb12("z_sb", [128, 16], F32)

        sPP = Sem(ctx, nc, "sPP")
        sXa = [Sem(ctx, nc, "sXa0"), Sem(ctx, nc, "sXa1")]
        sX2r = [Sem(ctx, nc, f"sX2r{q}") for q in range(8)]
        sBr = Sem(ctx, nc, "sBr")
        sBo = Sem(ctx, nc, "sBo")
        sCsl = Sem(ctx, nc, "sCsl")
        sStZ = Sem(ctx, nc, "sStZ")
        sStX = Sem(ctx, nc, "sStX")
        sStL = Sem(ctx, nc, "sStL")
        sStC = Sem(ctx, nc, "sStC")
        sStE = Sem(ctx, nc, "sStE")
        sOutP = [Sem(ctx, nc, f"sOut{q}") for q in range(8)]
        sStg = [Sem(ctx, nc, "sStg0"), Sem(ctx, nc, "sStg1")]
        sCst = Sem(ctx, nc, "sCst")
        sWgt = Sem(ctx, nc, "sWgt")
        sXb = [Sem(ctx, nc, "sXb0"), Sem(ctx, nc, "sXb1")]
        sCnL = Sem(ctx, nc, "sCnL")
        sW2 = Sem(ctx, nc, "sW2")
        sW1 = [Sem(ctx, nc, "sW10"), Sem(ctx, nc, "sW11")]
        sSc = Sem(ctx, nc, "sSc")
        sGt = [Sem(ctx, nc, f"sGt{q}") for q in range(4)]
        sEr = [Sem(ctx, nc, f"sEr{q}") for q in range(8)]
        sPE = Sem(ctx, nc, "sPE")
        sACT = Sem(ctx, nc, "sACT")
        sDVE = Sem(ctx, nc, "sDVE")
        sGP = Sem(ctx, nc, "sGP")
        ccCnt = Sem(ctx, nc, "ccCnt")
        ccTok = [Sem(ctx, nc, f"ccTok{q}") for q in range(max(1, N // (1024 if N % 1024 == 0 else N)))]
        ccSrc = Sem(ctx, nc, "ccSrc")
        ccEout = [Sem(ctx, nc, f"ccEout{q}") for q in range(max(1, CAP // (1024 if CAP % 1024 == 0 else CAP)))]

        block = ctx.enter_context(nc.Block())

        ps1 = ctx.enter_context(nc.psum_tensor("ps1", [128, 512], F32))
        ps2 = ctx.enter_context(nc.psum_tensor("ps2", [128, 512], F32))
        ps3 = ctx.enter_context(nc.psum_tensor("ps3", [128, 512], F32))
        ps4 = ctx.enter_context(nc.psum_tensor("ps4", [128, 512], F32))
        pstr = ctx.enter_context(nc.psum_tensor("pstr", [128, 1024], BF16))
        psLnA = ctx.enter_context(nc.psum_tensor("psLnA", [128, 512], F32))
        psLnB = ctx.enter_context(nc.psum_tensor("psLnB", [128, 512], F32))
        psE = ctx.enter_context(nc.psum_tensor("psE", [128, 512], F32))

        T = {
            "ps1": (sDVE, 0), "ps2": (sDVE, 0), "ps3": (sDVE, 0), "ps4": (sDVE, 0),
            "pstr": (sDVE, 0), "psLnA": (sDVE, 0), "psLnB": (sDVE, 0), "psE": (sDVE, 0),
            "sqbf": (sPE, 0), "cbf": (sPE, 0), "strow": (sPE, 0),
        }

        def twait(eng, *keys):
            for k in keys:
                s, v = T[k]
                eng.wait_ge(s.h, v)

        def vedge(v, instr):
            sDVE.inc(instr)
            sDVE.wait(v)
            return instr

        class StagedCaster:

            def __init__(self, stg, half=4096):
                self.stg = stg
                self.half = half
                self.cast_n = [0, 0]
                self.k = 0

            def emit(self, src_ap, dst_ap, nch, sync_waits=(), cast_waits=(),
                     eng="v"):
                par = self.k % 2
                self.k += 1
                sview = self.stg[:, par * self.half:(par + 1) * self.half]
                csem = sACT if eng == "s" else sDVE

                @block.sync
                def _(sy, par=par):
                    for sm, vl in sync_waits:
                        sy.wait_ge(sm.h, vl)
                    if self.cast_n[par]:
                        cs, cv = self.cast_n[par]
                        sy.wait_ge(cs.h, cv)
                    sStg[par].incd(sy.dma_start(
                        out=sview.rearrange("p (c co) -> p c co", c=nch),
                        in_=src_ap))

                ld = sStg[par].n

                if eng == "s":
                    @block.scalar
                    def _(s, par=par, ld=ld):
                        s.wait_ge(sStg[par].h, ld)
                        for sm, vl in cast_waits:
                            s.wait_ge(sm.h, vl)
                        sACT.inc(s.activation(dst_ap, sview, AF.Copy,
                                              bias=0.0, scale=1.0))
                else:
                    @block.vector
                    def _(v, par=par, ld=ld):
                        v.wait_ge(sStg[par].h, ld)
                        for sm, vl in cast_waits:
                            v.wait_ge(sm.h, vl)
                        sDVE.inc(v.tensor_copy(dst_ap, sview))

                self.cast_n[par] = (csem, csem.n)
                return (csem, csem.n)

        @block.sync
        def _(sy):
            sPP.incd(sy.dma_start(out=pp[:], in_=pp_d[:]))
            sPP.incd(sy.dma_start(out=consts[:], in_=consts_d[:]))

        p0_ls = sPP.n

        bcreg = []

        @block.gpsimd
        def _(g):
            r1 = g.alloc_register()
            g.reg_mov(r1, E * CAP - 1)
            r2 = g.alloc_register()
            g.reg_mov(r2, NCORES * N - 1)
            r3 = g.alloc_register()
            g.reg_mov(r3, E * CAPB - 1)
            bcreg.extend([r1, r2, r3])
            sCst.incd(g.dma_start(out=idn_bf[:], in_=consts_d[:, 0:128]))
            sCst.incd(g.dma_start(out=bq_row[:], in_=rows_d[0:1, :]))
            sCst.incd(g.dma_start(out=bv_row[:], in_=rows_d[1:2, :]))
            g.iota(iota8_row[:], pattern=[[1, E]], base=0, channel_multiplier=0,
                   allow_small_or_imprecise_dtypes=True)
            sGP.inc(g.iota(carr[:, 0:1], pattern=[[1, 1]], base=0, channel_multiplier=0,
                           allow_small_or_imprecise_dtypes=True))

        p0_gp = sGP.n
        p0_lg = sCst.n

        @block.vector
        def _(v):
            v.memset(ones_col_bf[:], 1.0)
            v.memset(ones_row_bf[:], 1.0)
            v.memset(ones_t[:], 1.0)
            v.memset(ones_col_f[:], 1.0)
            v.memset(one_f[:], 1.0)
            v.memset(eps_t[:], EPS)
            v.memset(zrow_bf[:], 0.0)
            sDVE.inc(v.memset(zero_f[:], 0.0))

        p0_dve = sDVE.n

        @block.sync
        def _(sy):
            sy.wait_ge(sDVE.h, p0_dve)
            sv = src_cin.rearrange("(b p) o -> b (p o)", p=128)
            nb = (E * CAP) // 128
            step = 128
            for b0 in range(0, nb, step):
                bn = min(step, nb - b0)
                sStZ.incd(sy.dma_start(out=sv[b0:b0 + bn, :],
                                      in_=zero_f[0:bn, 0:128]))

        src_zero_st = sStZ.n

        @block.sync
        def _(sy):
            sPP.incd(sy.dma_start(out=gidx_all[:], in_=gidx_d[:]))

        p0_ls = sPP.n
        gidx_done = None

        @block.tensor
        def _(t):
            t.wait_ge(sDVE.h, p0_dve)
            t.wait_ge(sPP.h, p0_ls)
            t.wait_ge(sCst.h, p0_lg)

        @block.scalar
        def _(s):
            s.wait_ge(sDVE.h, p0_dve)
            s.wait_ge(sPP.h, p0_ls)
            s.wait_ge(sCst.h, p0_lg)

        @block.vector
        def _(v):
            v.wait_ge(sPP.h, p0_ls)
            v.wait_ge(sCst.h, p0_lg)

        def LN_fm(xf, emit_out, src_bf=None, pre_sc=None, pre_dv=None):
            @block.scalar
            def _(s):
                if pre_sc:
                    for sm, vl in pre_sc:
                        s.wait_ge(sm.h, vl)
                twait(s, "sqbf")
                for c in range(NCH):
                    _li = s.activation(sqbf[:, c * TF:(c + 1) * TF], xf(c),
                                 AF.Square, bias=0.0, scale=1.0)
                sACT.inc(_li)

            a_sq = sACT.n

            if src_bf is None:
                @block.vector
                def _(v):
                    if pre_dv:
                        for sm, vl in pre_dv:
                            v.wait_ge(sm.h, vl)
                    twait(v, "cbf")
                    for c in range(NCH):
                        _li = v.tensor_copy(cbf[:, c * TF:(c + 1) * TF], xf(c))
                    sDVE.inc(_li)
                d_cb = sDVE.n
                ssrc = lambda c: cbf[:, c * TF:(c + 1) * TF]
            else:
                if pre_dv:
                    @block.vector
                    def _(v):
                        for sm, vl in pre_dv:
                            v.wait_ge(sm.h, vl)
                d_cb = None
                ssrc = src_bf

            @block.tensor
            def _(t):
                sACT.wait(t, a_sq)
                if d_cb is not None:
                    sDVE.wait(t, d_cb)
                twait(t, "psLnA", "psLnB", "strow")
                for c in range(NCH):
                    t.matmul(psLnA[0:1, 0:TF], ones_col_bf[:], ssrc(c),
                             start=(c == 0), stop=(c == NCH - 1))
                for c in range(NCH):
                    _li = t.matmul(psLnB[0:1, 0:TF], ones_col_bf[:],
                             sqbf[:, c * TF:(c + 1) * TF],
                             start=(c == 0), stop=(c == NCH - 1))
                sPE.inc(_li)

            p_sums = sPE.n
            T["sqbf"] = (sPE, p_sums)
            if src_bf is None:
                T["cbf"] = (sPE, p_sums)

            @block.vector
            def _(v):
                sPE.wait(v, p_sums)
                mu, va = strow[:, 0:TF], strow[:, TF:2 * TF]
                aa = strow[:, 2 * TF:3 * TF]
                v.tensor_scalar(mu, psLnA[0:1, 0:TF], 1.0 / C, None, op0=ALU.mult)
                vedge(v, v.tensor_scalar(va, psLnB[0:1, 0:TF], 1.0 / C, None, op0=ALU.mult))
                vedge(v, v.tensor_tensor(aa, mu, mu, op=ALU.mult))
                sDVE.inc(v.tensor_tensor(va, va, aa, op=ALU.subtract))

            d_var = sDVE.n

            @block.scalar
            def _(s):
                sDVE.wait(s, d_var)
                _li = s.activation(strow[:, 2 * TF:3 * TF], strow[:, TF:2 * TF],
                             AF.Ln, bias=eps1[:], scale=1.0)
                sACT.inc(_li)

            a_ln = sACT.n

            @block.scalar
            def _(s):
                sACT.wait(s, a_ln)
                sACT.inc(s.activation(strow[:, TF:2 * TF], strow[:, 2 * TF:3 * TF],
                                      AF.Exp, bias=0.0, scale=-0.5))

            a_rstd = sACT.n

            @block.tensor
            def _(t):
                sACT.wait(t, a_rstd)
                t.matmul(psLnA[:, 0:TF], ones_row_f[:], strow[:, 0:TF],
                         start=True, stop=True)
                _li = t.matmul(psLnB[:, 0:TF], ones_row_f[:], strow[:, TF:2 * TF],
                         start=True, stop=True)
                sPE.inc(_li)

            p_bc = sPE.n
            T["strow"] = (sPE, p_bc)

            @block.vector
            def _(v):
                sPE.wait(v, p_bc)
                last = emit_out(v, psLnA[:, 0:TF], psLnB[:, 0:TF], vedge)
                sDVE.inc(last)

            done = sDVE.n
            T["psLnA"] = (sDVE, done)
            T["psLnB"] = (sDVE, done)
            return done

        with ExitStack() as p1ctx:
            sb1 = lambda n_, s_, d_: p1ctx.enter_context(nc.sbuf_tensor(n_, s_, d_))
            xb = [sb1(f"xb{i}", [128, NCH * TF], BF16) for i in range(2)]
            ln1 = [sb1(f"ln1_{i}", [128, NCH * TF], BF16) for i in range(2)]
            wq_sb = sb1("wq_sb", [128, NCH * C], BF16)
            wk_sb = sb1("wk_sb", [128, NCH * C], BF16)
            wv_sb = sb1("wv_sb", [128, NCH * C], BF16)
            ek_fm = sb1("ek_fm", [128, 2 * TF], BF16)
            ek_tm = sb1("ek_tm", [128, 4 * C], BF16)
            v_bf = sb1("v_bf", [128, 2 * C], BF16)
            eq_s = sb1("eq_s", [128, C], F32)
            qrow = sb1("qrow", [128, 2 * H], F32)
            qs_tm = sb1("qs_tm", [128, 2 * C], BF16)
            zred = sb1("zred", [128, 8], F32)

            @block.gpsimd
            def _(g):
                for w_d, w_s in [(wqT, wq_sb), (wkT, wk_sb), (wvT, wv_sb)]:
                    srcv = w_d.rearrange("(c p) co -> p c co", p=128)
                    dstv = w_s[:].rearrange("p (c co) -> p c co", c=NCH)
                    for i in range(2):
                        sWgt.incd(g.dma_start(out=dstv[:, i * 4:(i + 1) * 4, :],
                                              in_=srcv[:, i * 4:(i + 1) * 4, :]))

            w_attn_ready = (sWgt, sWgt.n)

            xload = {}

            @block.gpsimd
            def _(g):
                for t in range(min(2, NT)):
                    src = xT.rearrange("(c p) n -> p c n", p=128)[:, :, t * TF:(t + 1) * TF]
                    dst = xb[t % 2][:].rearrange("p (c n) -> p c n", c=NCH)
                    sXb[t % 2].incd(g.dma_start(out=dst, in_=src))
                    xload[t] = (sXb[t % 2], sXb[t % 2].n)

            Tp = {
                "ln1_0": (sPE, 0), "ln1_1": (sPE, 0),
                "ekfm0": [(sPE, 0)], "ekfm1": [(sPE, 0)],
                "ektm": (sPE, 0),
                "vbf0": (sPE, 0), "vbf1": (sPE, 0),
                "eq": (sDVE, 0),
                "qstm0": (sPE, 0), "qstm1": (sPE, 0),
            }

            def tpwait(eng, *keys):
                for k in keys:
                    s, v = Tp[k]
                    eng.wait_ge(s.h, v)

            ln_a_rstd = {}

            def ln1_front(t):
                xt = xb[t % 2]

                @block.scalar
                def _(s, t=t):
                    xs, xv = xload[t]
                    s.wait_ge(xs.h, xv)
                    twait(s, "sqbf")
                    for c in range(NCH):
                        _li = s.activation(sqbf[:, c * TF:(c + 1) * TF],
                                           xt[:, c * TF:(c + 1) * TF],
                                           AF.Square, bias=0.0, scale=1.0)
                    sACT.inc(_li)

                a_sq = sACT.n

                @block.tensor
                def _(tt, t=t):
                    sACT.wait(tt, a_sq)
                    twait(tt, "psLnA", "psLnB", "strow")
                    for c in range(NCH):
                        tt.matmul(psLnA[0:1, 0:TF], ones_col_bf[:],
                                  xt[:, c * TF:(c + 1) * TF],
                                  start=(c == 0), stop=(c == NCH - 1))
                    for c in range(NCH):
                        _li = tt.matmul(psLnB[0:1, 0:TF], ones_col_bf[:],
                                        sqbf[:, c * TF:(c + 1) * TF],
                                        start=(c == 0), stop=(c == NCH - 1))
                    sPE.inc(_li)

                p_sums = sPE.n
                T["sqbf"] = (sPE, p_sums)

                @block.vector
                def _(v):
                    sPE.wait(v, p_sums)
                    mu, va = strow[:, 0:TF], strow[:, TF:2 * TF]
                    aa = strow[:, 2 * TF:3 * TF]
                    v.tensor_scalar(mu, psLnA[0:1, 0:TF], 1.0 / C, None, op0=ALU.mult)
                    vedge(v, v.tensor_scalar(va, psLnB[0:1, 0:TF], 1.0 / C, None, op0=ALU.mult))
                    vedge(v, v.tensor_tensor(aa, mu, mu, op=ALU.mult))
                    sDVE.inc(v.tensor_tensor(va, va, aa, op=ALU.subtract))

                d_var = sDVE.n

                @block.scalar
                def _(s):
                    sDVE.wait(s, d_var)
                    _li = s.activation(strow[:, 2 * TF:3 * TF], strow[:, TF:2 * TF],
                                       AF.Ln, bias=eps1[:], scale=1.0)
                    sACT.inc(_li)

                a_ln = sACT.n

                @block.scalar
                def _(s):
                    sACT.wait(s, a_ln)
                    _li = s.activation(strow[:, TF:2 * TF], strow[:, 2 * TF:3 * TF],
                                       AF.Exp, bias=0.0, scale=-0.5)
                    sACT.inc(_li)

                ln_a_rstd[t] = sACT.n

            def ln1_back(t):
                par = t % 2
                xt = xb[par]
                lnd = ln1[par]

                @block.tensor
                def _(tt, t=t):
                    sACT.wait(tt, ln_a_rstd[t])
                    tt.matmul(psLnA[:, 0:TF], ones_row_f[:], strow[:, 0:TF],
                              start=True, stop=True)
                    _li = tt.matmul(psLnB[:, 0:TF], ones_row_f[:], strow[:, TF:2 * TF],
                                    start=True, stop=True)
                    sPE.inc(_li)

                p_bc = sPE.n
                T["strow"] = (sPE, p_bc)

                @block.vector
                def _(v, t=t, par=par):
                    sPE.wait(v, p_bc)
                    sm, vl = Tp[f"ln1_{par}"]
                    v.wait_ge(sm.h, vl)
                    twait(v, "cbf")
                    for c in range(NCH):
                        ins = v.tensor_tensor(cbf[:, c * TF:(c + 1) * TF],
                                              xt[:, c * TF:(c + 1) * TF],
                                              psLnA[:, 0:TF], op=ALU.subtract)
                    vedge(v, ins)
                    for c in range(NCH):
                        ins = v.tensor_tensor(lnd[:, c * TF:(c + 1) * TF],
                                              cbf[:, c * TF:(c + 1) * TF],
                                              psLnB[:, 0:TF], op=ALU.mult)
                    sDVE.inc(ins)

                done = sDVE.n
                T["psLnA"] = (sDVE, done)
                T["psLnB"] = (sDVE, done)
                T["cbf"] = (sDVE, done)
                return done

            ln1_front(0)
            ln1_done = {0: ln1_back(0)}
            z_acc_n = 0

            for t in range(NT):
                par_t = t % 2
                lnt_bf = ln1[par_t]

                if t + 2 < NT:
                    @block.gpsimd
                    def _(g, t=t, lv=ln1_done[t]):
                        g.wait_ge(sDVE.h, lv)
                        src = xT.rearrange("(c p) n -> p c n", p=128)[:, :, (t + 2) * TF:(t + 3) * TF]
                        dst = xb[t % 2][:].rearrange("p (c n) -> p c n", c=NCH)
                        sXb[t % 2].incd(g.dma_start(out=dst, in_=src))
                        xload[t + 2] = (sXb[t % 2], sXb[t % 2].n)

                k_act = {}
                for co in range(NCH):
                    bank = ps1 if co % 2 == 0 else ps2
                    bkey = "ps1" if co % 2 == 0 else "ps2"

                    @block.tensor
                    def _(tt, co=co, bank=bank, bkey=bkey, t=t, lnt_bf=lnt_bf):
                        if co == 0:
                            tt.wait_ge(w_attn_ready[0].h, w_attn_ready[1])
                            sDVE.wait(tt, ln1_done[t])
                        twait(tt, bkey)
                        for ci in range(NCH):
                            _li = tt.matmul(bank[:, 0:TF],
                                      wk_sb[:, ci * C + co * 128:ci * C + (co + 1) * 128],
                                      lnt_bf[:, ci * TF:(ci + 1) * TF],
                                      start=(ci == 0), stop=(ci == NCH - 1))
                        sPE.inc(_li)

                    k_pe = sPE.n

                    @block.scalar
                    def _(s, co=co, bank=bank, k_pe=k_pe):
                        sPE.wait(s, k_pe)
                        for sm, vl in Tp[f"ekfm{co % 2}"]:
                            s.wait_ge(sm.h, vl)
                        sACT.inc(s.activation(
                            ek_fm[:, (co % 2) * TF:(co % 2 + 1) * TF], bank[:, 0:TF],
                            AF.Exp, bias=pp[:, PP_BK + co:PP_BK + co + 1], scale=1.0))

                    k_act[co] = sACT.n
                    T[bkey] = (sACT, sACT.n)

                    @block.tensor
                    def _(tt, co=co, t=t):
                        sACT.wait(tt, k_act[co])
                        twait(tt, "pstr")
                        if co == 0:
                            tpwait(tt, "ektm")
                        for s4 in range(4):
                            slot = (co % 2) * 4 + s4
                            _li = tt.transpose(pstr[:, slot * 128:(slot + 1) * 128],
                                         ek_fm[:, (co % 2) * TF + s4 * 128:(co % 2) * TF + (s4 + 1) * 128],
                                         idn_bf[:])
                        sPE.inc(_li)

                    k_tr = sPE.n

                    @block.vector
                    def _(v, co=co, k_tr=k_tr, z_acc_n=z_acc_n):
                        sPE.wait(v, k_tr)
                        if z_acc_n:
                            v.wait_ge(sDVE.h, z_acc_n)
                        for s4 in range(4):
                            slot = (co % 2) * 4 + s4
                            v.tensor_copy(ek_tm[:, s4 * C + co * 128:s4 * C + (co + 1) * 128],
                                          pstr[:, slot * 128:(slot + 1) * 128])
                        _li = v.tensor_reduce(
                            zred[:, co:co + 1].rearrange("p (a o) -> p a o", o=1),
                            ek_fm[:, (co % 2) * TF:(co % 2 + 1) * TF]
                            .rearrange("p (a n) -> p a n", a=1),
                            axis=AX.X, op=ALU.add)
                        sDVE.inc(_li)

                    Tp[f"ekfm{co % 2}"] = [(sPE, k_tr), (sDVE, sDVE.n)]
                    T["pstr"] = (sDVE, sDVE.n)
                ektm_ready = sDVE.n

                @block.vector
                def _(v, t=t, ektm_ready=ektm_ready):
                    sDVE.wait(v, ektm_ready)
                    if t == 0:
                        _li = v.tensor_copy(z_sb[:, 0:8], zred[:])
                    else:
                        _li = v.tensor_tensor(z_sb[:, 0:8], z_sb[:, 0:8], zred[:],
                                              op=ALU.add)
                    sDVE.inc(_li)

                z_acc_n = sDVE.n

                if t + 1 < NT:
                    ln1_front(t + 1)

                def emit_qs_tr_state(s4, qs_done, v_evac, t=t):
                    gs = t * 4 + s4
                    par = s4 % 2

                    @block.tensor
                    def _(tt, s4=s4, qs_done=qs_done):
                        sDVE.wait(tt, qs_done)
                        twait(tt, "pstr")
                        for c in range(NCH):
                            _li = tt.transpose(pstr[:, c * 128:(c + 1) * 128],
                                         qs_tm[:, par * C + c * 128:par * C + (c + 1) * 128],
                                         idn_bf[:])
                        sPE.inc(_li)

                    qtr_pe = sPE.n
                    Tp[f"qstm{par}"] = (sPE, qtr_pe)

                    @block.vector
                    def _(v, gs=gs, qtr_pe=qtr_pe):
                        sPE.wait(v, qtr_pe)
                        for c in range(NCH):
                            _li = v.tensor_copy(qs_fm[:, c * N + gs * 128:c * N + (gs + 1) * 128],
                                          pstr[:, c * 128:(c + 1) * 128])
                        sDVE.inc(_li)

                    T["pstr"] = (sDVE, sDVE.n)

                    @block.tensor
                    def _(tt, s4=s4, gs=gs, par=par, v_evac=v_evac):
                        sDVE.wait(tt, v_evac)
                        if s4 == 0:
                            sDVE.wait(tt, ektm_ready)
                        first, last = (gs == 0), (gs == NSUB - 1)
                        if first:
                            tt.matmul(psE[:, 0:512], ones_row_bf[:], zrow_bf[:],
                                      start=True, stop=False)
                        for h in range(H):
                            hb = (h % 2) * 64
                            hc = (h // 2) * D
                            _li = tt.matmul(psE[hb:hb + 64, hc:hc + D],
                                      ek_tm[:, s4 * C + h * D:s4 * C + (h + 1) * D],
                                      v_bf[:, par * C + h * D:par * C + (h + 1) * D],
                                      start=False, stop=False)
                        if last:
                            _li = tt.matmul(psE[:, 0:512], ones_row_bf[:], zrow_bf[:],
                                            start=False, stop=True)
                        sPE.inc(_li)

                    Tp[f"vbf{par}"] = (sPE, sPE.n)
                    Tp["ektm"] = (sPE, sPE.n)

                pend = None
                for s4 in range(4):
                    par = s4 % 2

                    @block.tensor
                    def _(tt, s4=s4, par=par, lnt_bf=lnt_bf):
                        twait(tt, "ps3", "ps4")
                        sm, vl = Tp[f"vbf{par}"]
                        tt.wait_ge(sm.h, vl)
                        for half, bank in ((0, ps3), (1, ps4)):
                            for ci in range(NCH):
                                tt.matmul(bank[:, 0:512],
                                          lnt_bf[:, ci * TF + s4 * 128:ci * TF + (s4 + 1) * 128],
                                          wv_sb[:, ci * C + half * 512:ci * C + (half + 1) * 512],
                                          start=(ci == 0), stop=False)
                            _li = tt.matmul(bank[:, 0:512], ones_row_bf[:],
                                      bv_row[:, half * 512:(half + 1) * 512],
                                      start=False, stop=True)
                        sPE.inc(_li)

                    v_pe = sPE.n

                    @block.vector
                    def _(v, par=par, v_pe=v_pe):
                        sPE.wait(v, v_pe)
                        sm, vl = Tp[f"vbf{par}"]
                        v.wait_ge(sm.h, vl)
                        v.tensor_copy(v_bf[:, par * C:par * C + 512], ps3[:, 0:512])
                        _li = v.tensor_copy(v_bf[:, par * C + 512:par * C + 1024], ps4[:, 0:512])
                        sDVE.inc(_li)

                    v_evac = sDVE.n
                    T["ps3"] = (sDVE, v_evac)
                    T["ps4"] = (sDVE, v_evac)

                    @block.tensor
                    def _(tt, s4=s4, lnt_bf=lnt_bf):
                        twait(tt, "ps1", "ps2")
                        for half, bank in ((0, ps1), (1, ps2)):
                            for ci in range(NCH):
                                tt.matmul(bank[:, 0:512],
                                          lnt_bf[:, ci * TF + s4 * 128:ci * TF + (s4 + 1) * 128],
                                          wq_sb[:, ci * C + half * 512:ci * C + (half + 1) * 512],
                                          start=(ci == 0), stop=False)
                            _li = tt.matmul(bank[:, 0:512], ones_row_bf[:],
                                      bq_row[:, half * 512:(half + 1) * 512],
                                      start=False, stop=True)
                        sPE.inc(_li)

                    q_pe = sPE.n

                    @block.scalar
                    def _(s, q_pe=q_pe):
                        sPE.wait(s, q_pe)
                        sm, vl = Tp["eq"]
                        s.wait_ge(sm.h, vl)
                        s.activation(eq_s[:, 0:512], ps1[:, 0:512], AF.Exp, bias=0.0, scale=1.0)
                        _li = s.activation(eq_s[:, 512:1024], ps2[:, 0:512], AF.Exp, bias=0.0, scale=1.0)
                        sACT.inc(_li)

                    q_exp = sACT.n
                    T["ps1"] = (sACT, q_exp)
                    T["ps2"] = (sACT, q_exp)

                    @block.vector
                    def _(v, par=par, q_exp=q_exp):
                        sACT.wait(v, q_exp)
                        sm, vl = Tp[f"qstm{par}"]
                        v.wait_ge(sm.h, vl)
                        eq3 = eq_s[:].rearrange("p (h d) -> p h d", d=D)
                        vedge(v, v.tensor_reduce(qrow[:, 0:H].rearrange("p (h o) -> p h o", o=1),
                                                 eq3, axis=AX.X, op=ALU.add))
                        vedge(v, v.reciprocal(qrow[:, H:2 * H], qrow[:, 0:H]))
                        qr3 = qrow[:, H:2 * H].rearrange("p (h o) -> p h o", o=1)
                        _li = v.tensor_tensor(
                            qs_tm[:, par * C:(par + 1) * C].rearrange("p (h d) -> p h d", d=D),
                            eq3, qr3.to_broadcast([128, H, D]), op=ALU.mult)
                        sDVE.inc(_li)

                    qs_done = sDVE.n
                    Tp["eq"] = (sDVE, qs_done)

                    if pend is not None:
                        emit_qs_tr_state(*pend)
                    pend = (s4, qs_done, v_evac)

                    if s4 == 1 and t + 1 < NT:
                        ln1_done[t + 1] = ln1_back(t + 1)

                emit_qs_tr_state(*pend)
                Tp[f"ln1_{par_t}"] = (sPE, sPE.n)

            pass1_pe = sPE.n
            pass1_dve = sDVE.n

            @block.vector
            def _(v):
                sPE.wait(v, pass1_pe)
                sDVE.wait(v, pass1_dve)
                vedge(v, v.reciprocal(z_sb[:, 8:16], z_sb[:, 0:8]))
                for h in range(H):
                    hb = (h % 2) * 64
                    hc = (h // 2) * D
                    _li = v.tensor_scalar(s_bf[hb:hb + 64, hc:hc + D], psE[hb:hb + 64, hc:hc + D],
                                    z_sb[hb:hb + 64, 8 + h // 2:8 + h // 2 + 1],
                                    None, op0=ALU.mult)
                sDVE.inc(_li)

            epi_done = sDVE.n
            T["psE"] = (sDVE, epi_done)

        with ExitStack() as p2ctx:
            sb2 = lambda n_, s_, d_: p2ctx.enter_context(nc.sbuf_tensor(n_, s_, d_))
            wo_sb = sb2("wo_sb", [128, NCH * C], BF16)
            attno2 = [sb2(f"attno{i}", [128, NCH * TF], F32) for i in range(2)]
            cbfA = [sb2(f"cbfA{i}", [128, NCH * TF], BF16) for i in range(2)]
            lnb = sb2("lnb", [128, NCH * TF], BF16)
            x2f = sb2("x2f", [128, NCH * TF], F32)
            xa3 = sb2("xa3", [128, 2 * TF], F32)
            x2tm = sb2("x2tm", [128, 2 * C], BF16)
            lntm = sb2("lntm", [128, 2 * C], BF16)
            lgrow = sb2("lgrow", [8, TF], F32)
            ones_row5_bf = sb2("ones_row5_bf", [1, TF], BF16)
            bcol_bf = sb2("bcol_bf", [128, 8], BF16)
            wob_row = sb2("wob_row", [1, C], BF16)

            @block.gpsimd
            def _(g):
                g.wait_ge(sPE.h, pass1_pe)
                srcwo = woT.rearrange("(c p) co -> p c co", p=128)
                dstwo = wo_sb[:].rearrange("p (c co) -> p c co", c=NCH)
                for i in range(2):
                    sWgt.incd(g.dma_start(out=dstwo[:, i * 4:(i + 1) * 4, :],
                                          in_=srcwo[:, i * 4:(i + 1) * 4, :]))

            wo_ready = (sWgt, sWgt.n)

            @block.vector
            def _(v):
                v.wait_ge(sDVE.h, epi_done)
                v.memset(ones_row5_bf[:], 1.0)
                _li = v.tensor_copy(bcol_bf[:], pp[:, PP_LNB:PP_LNB + 8])
                sDVE.inc(_li)

            p2pre_dve = sDVE.n

            wob_done = 0
            for rnd in range(2):
                @block.tensor
                def _(tt, rnd=rnd):
                    if rnd == 0:
                        tt.wait_ge(wo_ready[0].h, wo_ready[1])
                        sDVE.wait(tt, p2pre_dve)
                    twait(tt, "psE")
                    for co4 in range(4):
                        co = rnd * 4 + co4
                        for ci in range(NCH):
                            _li = tt.matmul(psE[0:1, co4 * 128:(co4 + 1) * 128],
                                            bcol_bf[:, ci:ci + 1],
                                            wo_sb[:, ci * C + co * 128:ci * C + (co + 1) * 128],
                                            start=(ci == 0), stop=(ci == NCH - 1))
                    sPE.inc(_li)

                p_wob = sPE.n

                @block.vector
                def _(v, rnd=rnd, p_wob=p_wob):
                    sPE.wait(v, p_wob)
                    _li = v.tensor_copy(wob_row[:, rnd * 512:(rnd + 1) * 512],
                                        psE[0:1, 0:512])
                    sDVE.inc(_li)

                wob_done = sDVE.n
                T["psE"] = (sDVE, wob_done)

            @block.vector
            def _(v):
                sDVE.wait(v, wob_done)
                for ci in range(NCH):
                    _li = v.tensor_scalar(wo_sb[:, ci * C:(ci + 1) * C],
                                          wo_sb[:, ci * C:(ci + 1) * C],
                                          pp[:, PP_LNG + ci:PP_LNG + ci + 1],
                                          None, op0=ALU.mult)
                sDVE.inc(_li)

            wo_scaled = sDVE.n

            Tq = {
                "attno0": (sPE, 0), "attno1": (sPE, 0),
                "cbfA0": [(sPE, 0)], "cbfA1": [(sPE, 0)],
                "lnb": (sPE, 0), "x2f": (sACT, 0), "x2f_dve": (sDVE, 0),
                "lgrow": (sPE, 0),
            }
            x2st = {}
            lnst = {}
            xchunk_load = {}
            ln2_norm = {}
            gate_pe_t = {}

            def tqwait(eng, *keys):
                for k in keys:
                    s, v = Tq[k]
                    eng.wait_ge(s.h, v)

            lna_state = {}
            ln2_state = {}

            def emit_A(t):
                par = t % 2
                att = cbfA[par]
                o_last = 0
                for co in range(NCH):
                    bank = ps1 if co % 2 == 0 else ps2
                    bkey = "ps1" if co % 2 == 0 else "ps2"

                    @block.tensor
                    def _(tt, co=co, bank=bank, bkey=bkey, t=t):
                        if t == 0 and co == 0:
                            sDVE.wait(tt, epi_done)
                        twait(tt, bkey)
                        for s4 in range(4):
                            for hh in range(2):
                                h = 2 * co + hh
                                hb = hh * 64
                                _li = tt.matmul(
                                    bank[hb:hb + 64, s4 * 128:(s4 + 1) * 128],
                                    s_bf[hb:hb + 64, (h // 2) * D:(h // 2 + 1) * D],
                                    qs_fm[hb:hb + 64,
                                          co * N + t * TF + s4 * 128:co * N + t * TF + (s4 + 1) * 128],
                                    start=True, stop=True)
                        sPE.inc(_li)

                    o_pe = sPE.n

                    @block.scalar
                    def _(s, co=co, bank=bank, o_pe=o_pe, par=par, att=att):
                        sPE.wait(s, o_pe)
                        if co == 0:
                            for sm, vl in Tq[f"cbfA{par}"]:
                                s.wait_ge(sm.h, vl)
                        _li = s.activation(att[:, co * TF:(co + 1) * TF], bank[:, 0:TF],
                                           AF.Copy, bias=0.0, scale=1.0)
                        sACT.inc(_li)

                    o_last = sACT.n
                    T[bkey] = (sACT, sACT.n)
                return o_last

            def emit_ln_front(t, att_f, do_cast, state, str_, bkA, bkB, kA, kB,
                              pre_sc, sum_src=None, epsap=None):
                epsap = epsap if epsap is not None else eps1
                """squares(+cast) ACT; sums PE; stats DVE; Ln/Exp ACT."""
                @block.scalar
                def _(s, t=t):
                    for sm, vl in pre_sc:
                        s.wait_ge(sm.h, vl)
                    twait(s, "sqbf")
                    if do_cast:
                        twait(s, "cbf")
                        for c in range(NCH):
                            s.activation(cbf[:, c * TF:(c + 1) * TF], att_f(c),
                                         AF.Copy, bias=0.0, scale=1.0)
                    for c in range(NCH):
                        _li = s.activation(sqbf[:, c * TF:(c + 1) * TF], att_f(c),
                                           AF.Square, bias=0.0, scale=1.0)
                    sACT.inc(_li)

                a_sq = sACT.n
                state["a_sq"] = a_sq
                ssrc = (lambda c: cbf[:, c * TF:(c + 1) * TF]) if do_cast else sum_src

                @block.tensor
                def _(tt):
                    sACT.wait(tt, a_sq)
                    twait(tt, kA, kB)
                    for c in range(NCH):
                        tt.matmul(bkA[0:1, 0:TF], ones_col_bf[:], ssrc(c),
                                  start=(c == 0), stop=(c == NCH - 1))
                    for c in range(NCH):
                        _li = tt.matmul(bkB[0:1, 0:TF], ones_col_bf[:],
                                        sqbf[:, c * TF:(c + 1) * TF],
                                        start=(c == 0), stop=(c == NCH - 1))
                    sPE.inc(_li)

                p_sums = sPE.n
                T["sqbf"] = (sPE, p_sums)
                if do_cast:
                    T["cbf"] = (sPE, p_sums)
                state["p_sums"] = p_sums

                @block.vector
                def _(v):
                    sPE.wait(v, p_sums)
                    mu, va = str_[:, 0:TF], str_[:, TF:2 * TF]
                    aa = str_[:, 2 * TF:3 * TF]
                    v.tensor_scalar(mu, bkA[0:1, 0:TF], 1.0 / C, None, op0=ALU.mult)
                    vedge(v, v.tensor_scalar(va, bkB[0:1, 0:TF], 1.0 / C, None, op0=ALU.mult))
                    vedge(v, v.tensor_tensor(aa, mu, mu, op=ALU.mult))
                    sDVE.inc(v.tensor_tensor(va, va, aa, op=ALU.subtract))

                d_var = sDVE.n

                @block.scalar
                def _(s):
                    sDVE.wait(s, d_var)
                    _li = s.activation(str_[:, 2 * TF:3 * TF], str_[:, TF:2 * TF],
                                       AF.Ln, bias=epsap[:], scale=1.0)
                    sACT.inc(_li)

                a_ln = sACT.n

                @block.scalar
                def _(s):
                    sACT.wait(s, a_ln)
                    _li = s.activation(str_[:, TF:2 * TF], str_[:, 2 * TF:3 * TF],
                                       AF.Exp, bias=0.0, scale=-0.5)
                    sACT.inc(_li)

                state["a_rstd"] = sACT.n

            def emit_ln_back(state, str_, bkA, bkB, emit_norm, onesap=None):
                onesap = onesap if onesap is not None else ones_row_f

                @block.tensor
                def _(tt):
                    sACT.wait(tt, state["a_rstd"])
                    tt.matmul(bkA[:, 0:TF], onesap[:], str_[:, 0:TF],
                              start=True, stop=True)
                    _li = tt.matmul(bkB[:, 0:TF], onesap[:], str_[:, TF:2 * TF],
                                    start=True, stop=True)
                    sPE.inc(_li)

                p_bc = sPE.n

                @block.vector
                def _(v):
                    sPE.wait(v, p_bc)
                    last = emit_norm(v, bkA[:, 0:TF], bkB[:, 0:TF])
                    sDVE.inc(last)

                return sDVE.n

            def emit_LNa_front(t, o_last):
                par = t % 2
                att = cbfA[par]
                lna_state[t] = {}
                emit_ln_front(t, lambda c: att[:, c * TF:(c + 1) * TF], False,
                              lna_state[t], strow, psLnA, psLnB, "psLnA", "psLnB",
                              pre_sc=[(sACT, o_last)],
                              sum_src=lambda c: att[:, c * TF:(c + 1) * TF])

            def emit_LNa_back(t):
                par = t % 2
                att = cbfA[par]

                def norm(v, mu, rstd):
                    tqwait(v, "lnb")
                    for c in range(NCH):
                        ins = v.tensor_tensor(lnb[:, c * TF:(c + 1) * TF],
                                              att[:, c * TF:(c + 1) * TF], mu,
                                              op=ALU.subtract)
                    vedge(v, ins)
                    for c in range(NCH):
                        ins = v.tensor_tensor(lnb[:, c * TF:(c + 1) * TF],
                                              lnb[:, c * TF:(c + 1) * TF], rstd,
                                              op=ALU.mult)
                    return ins

                done = emit_ln_back(lna_state[t], strow, psLnA, psLnB, norm)
                T["psLnA"] = (sDVE, done)
                T["psLnB"] = (sDVE, done)
                lna_state[t]["done"] = done
                Tq[f"cbfA{par}"] = [(sPE, lna_state[t]["p_sums"]), (sDVE, done)]

            def emit_B(t):
                x2_dve = {}
                for co in range(NCH):
                    bank = ps3 if co % 2 == 0 else ps4
                    bkey = "ps3" if co % 2 == 0 else "ps4"

                    @block.sync
                    def _(sy, co=co, t=t):
                        prev = (t, co - 2) if co >= 2 else (t - 1, co + NCH - 2)
                        if prev in xchunk_evac:
                            sy.wait_ge(sDVE.h, xchunk_evac[prev])
                        elif t == 0:
                            sy.wait_ge(sDVE.h, epi_done)
                        src = xT[co * 128:(co + 1) * 128, t * TF:(t + 1) * TF]
                        sXa[co % 2].incd(sy.dma_start(
                            out=xa3[:, (co % 2) * TF:(co % 2 + 1) * TF], in_=src))
                        xchunk_load[(t, co)] = (sXa[co % 2], sXa[co % 2].n)

                    @block.tensor
                    def _(tt, co=co, bank=bank, bkey=bkey, t=t):
                        if co == 0:
                            sDVE.wait(tt, wo_scaled)
                            sDVE.wait(tt, lna_state[t]["done"])
                        twait(tt, bkey)
                        for ci in range(NCH):
                            tt.matmul(bank[:, 0:TF],
                                      wo_sb[:, ci * C + co * 128:ci * C + (co + 1) * 128],
                                      lnb[:, ci * TF:(ci + 1) * TF],
                                      start=(ci == 0), stop=False)
                        _li = tt.matmul(bank[:, 0:TF],
                                        wob_row[:, co * 128:(co + 1) * 128],
                                        ones_row5_bf[:], start=False, stop=True)
                        sPE.inc(_li)

                    wo_pe = sPE.n
                    if co == NCH - 1:
                        Tq["lnb"] = (sPE, wo_pe)

                    @block.vector
                    def _(v, co=co, bank=bank, t=t, wo_pe=wo_pe):
                        sPE.wait(v, wo_pe)
                        xs, xv = xchunk_load[(t, co)]
                        v.wait_ge(xs.h, xv)
                        if co == 0:
                            tqwait(v, "x2f", "x2f_dve")
                        vedge(v, v.tensor_scalar(x2f[:, co * TF:(co + 1) * TF],
                                                 bank[:, 0:TF],
                                                 pp[:, PP_BO + co:PP_BO + co + 1],
                                                 None, op0=ALU.add))
                        _li = v.tensor_tensor(x2f[:, co * TF:(co + 1) * TF],
                                              x2f[:, co * TF:(co + 1) * TF],
                                              xa3[:, (co % 2) * TF:(co % 2 + 1) * TF],
                                              op=ALU.add)
                        sDVE.inc(_li)

                    x2_dve[(t, co)] = sDVE.n
                    xchunk_evac[(t, co)] = sDVE.n
                    T[bkey] = (sDVE, sDVE.n)
                return sDVE.n

            def emit_x2tr(t, x2_done):
                for s4 in range(4):
                    @block.tensor
                    def _(tt, s4=s4, x2_done=x2_done):
                        if s4 == 0:
                            sDVE.wait(tt, x2_done)
                        twait(tt, "ps1", "ps2")
                        for c in range(NCH):
                            bank = ps1 if c < 4 else ps2
                            _li = tt.transpose(bank[:, (c % 4) * 128:(c % 4 + 1) * 128],
                                         x2f[:, c * TF + s4 * 128:c * TF + (s4 + 1) * 128],
                                         consts[:, 0:128])
                        sPE.inc(_li)

                    tr_pe = sPE.n

                    @block.vector
                    def _(v, s4=s4, t=t, tr_pe=tr_pe):
                        sPE.wait(v, tr_pe)
                        b = t * 4 + s4
                        if b - 1 in x2st:
                            v.wait_ge(sStX.h, x2st[b - 1])
                        for c in range(NCH):
                            bank = ps1 if c < 4 else ps2
                            _li = v.tensor_copy(x2tm[:, (s4 % 2) * C + c * 128:(s4 % 2) * C + (c + 1) * 128],
                                          bank[:, (c % 4) * 128:(c % 4 + 1) * 128])
                        sDVE.inc(_li)

                    x2tr_d = sDVE.n
                    T["ps1"] = (sDVE, sDVE.n)
                    T["ps2"] = (sDVE, sDVE.n)

                    @block.scalar
                    def _(sc, s4=s4, t=t, x2tr_d=x2tr_d):
                        sc.wait_ge(sDVE.h, x2tr_d)
                        row0 = t * TF + s4 * 128
                        sStX.incd(sc.dma_start(out=x2_dram[row0:row0 + 128, :],
                                               in_=x2tm[:, (s4 % 2) * C:(s4 % 2 + 1) * C]))

                    x2st[t * 4 + s4] = sStX.n
                return sPE.n

            def emit_LN2_front(t, x2_done):
                ln2_state[t] = {}
                emit_ln_front(t, lambda c: x2f[:, c * TF:(c + 1) * TF], True,
                              ln2_state[t], strow2, ps3, ps4, "ps3", "ps4",
                              pre_sc=[(sDVE, x2_done)], epsap=eps1_64)
                Tq["x2f"] = (sACT, ln2_state[t]["a_sq"])

            def emit_LN2_back(t):
                par = t % 2
                att = attno2[par]

                def norm(v, mu, rstd):
                    tqwait(v, f"attno{par}")
                    for c in range(NCH):
                        ins = v.tensor_tensor(att[:, c * TF:(c + 1) * TF],
                                              x2f[:, c * TF:(c + 1) * TF], mu,
                                              op=ALU.subtract)
                    vedge(v, ins)
                    for c in range(NCH):
                        ins = v.tensor_tensor(att[:, c * TF:(c + 1) * TF],
                                              att[:, c * TF:(c + 1) * TF], rstd,
                                              op=ALU.mult)
                    return ins

                done = emit_ln_back(ln2_state[t], strow2, ps3, ps4, norm,
                                    onesap=ones_row_f64)
                T["ps3"] = (sDVE, done)
                T["ps4"] = (sDVE, done)
                ln2_norm[t] = done
                Tq["x2f_dve"] = (sDVE, done)

            def emit_gate(t):
                par = t % 2
                att = attno2[par]

                @block.tensor
                def _(tt, t=t):
                    sDVE.wait(tt, ln2_norm[t])
                    twait(tt, "psE")
                    for ci in range(NCH):
                        _li = tt.matmul(psE[0:8, 0:TF],
                                  pp[:, PP_WG + ci * 8:PP_WG + (ci + 1) * 8],
                                  att[:, ci * TF:(ci + 1) * TF],
                                  start=(ci == 0), stop=(ci == NCH - 1))
                    sPE.inc(_li)

                gate_pe = sPE.n
                gate_pe_t[t] = gate_pe
                Tq[f"attno{par}"] = (sPE, gate_pe)

                @block.vector
                def _(v, gate_pe=gate_pe):
                    sPE.wait(v, gate_pe)
                    tqwait(v, "lgrow")
                    _li = v.tensor_copy(lgrow[:], psE[0:8, 0:TF])
                    sDVE.inc(_li)

                lg_dve = sDVE.n

                @block.tensor
                def _(tt, lg_dve=lg_dve):
                    sDVE.wait(tt, lg_dve)
                    for s4 in range(4):
                        _li = tt.transpose(psE[:, 256 + s4 * 8:256 + (s4 + 1) * 8],
                                           lgrow[0:8, s4 * 128:(s4 + 1) * 128],
                                           consts[0:8, 0:8])
                    sPE.inc(_li)

                lgt_pe = sPE.n
                Tq["lgrow"] = (sPE, lgt_pe)

                @block.vector
                def _(v, t=t, lgt_pe=lgt_pe):
                    sPE.wait(v, lgt_pe)
                    _li = v.tensor_copy(glog[:, t * 4 * E:(t + 1) * 4 * E],
                                        psE[:, 256:288])
                    sDVE.inc(_li)

                T["psE"] = (sDVE, sDVE.n)

            def emit_ln2bf_tr(t):
                par = t % 2
                att = attno2[par]

                @block.scalar
                def _(s, t=t):
                    sDVE.wait(s, ln2_norm[t])
                    twait(s, "cbf")
                    for c in range(NCH):
                        _li = s.activation(cbf[:, c * TF:(c + 1) * TF],
                                     att[:, c * TF:(c + 1) * TF],
                                     AF.Copy, bias=0.0, scale=1.0)
                    sACT.inc(_li)

                ln2bf_act = sACT.n

                for s4 in range(4):
                    @block.tensor
                    def _(tt, s4=s4, ln2bf_act=ln2bf_act):
                        if s4 == 0:
                            sACT.wait(tt, ln2bf_act)
                        twait(tt, "pstr")
                        for c in range(NCH):
                            _li = tt.transpose(pstr[:, c * 128:(c + 1) * 128],
                                         cbf[:, c * TF + s4 * 128:c * TF + (s4 + 1) * 128],
                                         idn_bf[:])
                        sPE.inc(_li)

                    ltr_pe = sPE.n
                    if s4 == 3:
                        T["cbf"] = (sPE, ltr_pe)

                    @block.vector
                    def _(v, s4=s4, t=t, ltr_pe=ltr_pe):
                        sPE.wait(v, ltr_pe)
                        b = t * 4 + s4
                        if b - 1 in lnst:
                            v.wait_ge(sStL.h, lnst[b - 1])
                        for c in range(NCH):
                            _li = v.tensor_copy(lntm[:, (s4 % 2) * C + c * 128:(s4 % 2) * C + (c + 1) * 128],
                                          pstr[:, c * 128:(c + 1) * 128])
                        sDVE.inc(_li)

                    lntr_d = sDVE.n
                    T["pstr"] = (sDVE, sDVE.n)

                    @block.scalar
                    def _(sc, s4=s4, t=t, lntr_d=lntr_d):
                        sc.wait_ge(sDVE.h, lntr_d)
                        row0 = t * TF + s4 * 128
                        sStL.incd(sc.dma_start(out=ln2_cin[row0:row0 + 128, :],
                                               in_=lntm[:, (s4 % 2) * C:(s4 % 2 + 1) * C]))

                    lnst[t * 4 + s4] = sStL.n

                tpc = CHK_T // TF
                if (t + 1) % tpc == 0:
                    ck = (t + 1) // tpc - 1

                    @block.gpsimd
                    def _(g, ck=ck, lv=lnst[t * 4 + 3]):
                        g.wait_ge(sStL.h, lv)
                        ccTok[ck].inc(g.collective_compute(
                            "AllGather", ALU.bypass, replica_groups=RG,
                            ins=[ln2_cin[ck * CHK_T:(ck + 1) * CHK_T, :].opt()],
                            outs=[tok_all[ck * NCORES * CHK_T:(ck + 1) * NCORES * CHK_T, :].opt()]))

            xchunk_evac = {}
            x2_done_t = {}
            for t in range(NT):
                o_last = emit_A(t)
                emit_LNa_front(t, o_last)
                if t >= 1:
                    x2_done_t[t - 1] = emit_B(t - 1)
                    emit_x2tr(t - 1, x2_done_t[t - 1])
                    emit_LN2_front(t - 1, x2_done_t[t - 1])
                emit_LNa_back(t)
                if t >= 1:
                    emit_LN2_back(t - 1)
                    emit_gate(t - 1)
                    emit_ln2bf_tr(t - 1)

            tl = NT - 1
            x2_done_t[tl] = emit_B(tl)
            emit_x2tr(tl, x2_done_t[tl])
            emit_LN2_front(tl, x2_done_t[tl])
            emit_LN2_back(tl)
            emit_gate(tl)
            emit_ln2bf_tr(tl)

            pass2_x2st = sStX.n
            pass2_lnst = sStL.n
            pass2_dve = sDVE.n

            @block.vector
            def _(v):
                sDVE.wait(v, pass2_dve)
                g3 = glog[:].rearrange("p (s e) -> p s e", e=E)
                m = pos_s[:]
                vedge(v, v.tensor_reduce(m.rearrange("p (s o) -> p s o", o=1), g3,
                                         axis=AX.X, op=ALU.max))
                mb = m.rearrange("p (s o) -> p s o", o=1).to_broadcast([128, NSUB, E])
                t3 = scr[:].rearrange("p (s e) -> p s e", e=E)
                _li = v.tensor_tensor(t3, g3, mb, op=ALU.subtract)
                sDVE.inc(_li)

            g_dve1 = sDVE.n

            @block.scalar
            def _(s):
                sDVE.wait(s, g_dve1)
                sACT.inc(s.activation(base_bc[:], scr[:], AF.Exp, bias=0.0, scale=1.0))

            g_act = sACT.n

            @block.vector
            def _(v):
                sACT.wait(v, g_act)
                ex3 = base_bc[:].rearrange("p (s e) -> p s e", e=E)
                sm = gp_s[:]
                vedge(v, v.tensor_reduce(sm.rearrange("p (s o) -> p s o", o=1), ex3,
                                         axis=AX.X, op=ALU.add))
                vedge(v, v.reciprocal(grec[:], sm))
                g3 = glog[:].rearrange("p (s e) -> p s e", e=E)
                mb = pos_s[:].rearrange("p (s o) -> p s o", o=1).to_broadcast([128, NSUB, E])
                msk3 = scr[:].rearrange("p (s e) -> p s e", e=E)
                vedge(v, v.tensor_tensor(msk3, g3, mb, op=ALU.is_ge))
                vedge(v, v.tensor_scalar(oh_s[:], scr[:], -64.0, 64.0, op0=ALU.mult, op1=ALU.add))
                iob = iota8_row[:].rearrange("p (o e) -> p o e", o=1).to_broadcast([128, NSUB, E])
                cand3 = oh_s[:].rearrange("p (s e) -> p s e", e=E)
                vedge(v, v.tensor_tensor(cand3, cand3, iob, op=ALU.add))
                vedge(v, v.tensor_reduce(eidx[:].rearrange("p (s o) -> p s o", o=1), cand3,
                                         axis=AX.X, op=ALU.min))
                eb = eidx[:].rearrange("p (s o) -> p s o", o=1).to_broadcast([128, NSUB, E])
                oh3 = oh_s[:].rearrange("p (s e) -> p s e", e=E)
                _li = v.tensor_tensor(oh3, iob, eb, op=ALU.is_equal)
                sDVE.inc(_li)

            g_done = sDVE.n

            @block.tensor
            def _(tt):
                sDVE.wait(tt, g_done)
                twait(tt, "ps1", "ps2")
                for s in range(NSUB):
                    _li = tt.matmul(ps1[:, s * 8:(s + 1) * 8],
                              consts[:, 128:256], oh_s[:, s * E:(s + 1) * E],
                              start=True, stop=True)
                sPE.inc(_li)

            lp_pe = sPE.n

            @block.vector
            def _(v):
                sPE.wait(v, lp_pe)
                _li = v.tensor_copy(lpos_blk[:, 0:NSUB * E], ps1[:, 0:NSUB * E])
                sDVE.inc(_li)

            lp_dve = sDVE.n
            T["ps1"] = (sDVE, lp_dve)

            @block.tensor
            def _(tt):
                sDVE.wait(tt, g_done)
                twait(tt, "ps2")
                for s in range(NSUB):
                    _li = tt.matmul(ps2[0:8, s:s + 1], oh_s[:, s * E:(s + 1) * E],
                              ones_col_f[:], start=True, stop=True)
                sPE.inc(_li)

            cnt_pe = sPE.n

            @block.vector
            def _(v):
                sPE.wait(v, cnt_pe)
                v.wait_ge(sGP.h, p0_gp)
                for s in range(NSUB):
                    _li = v.tensor_tensor(carr[:, s + 1:s + 2], carr[:, s:s + 1],
                                          ps2[0:8, s:s + 1], op=ALU.add)
                    if s != NSUB - 1:
                        vedge(v, _li)
                sDVE.inc(_li)

            cum_done = sDVE.n
            T["ps2"] = (sDVE, cum_done)

            @block.sync
            def _(sy):
                sy.wait_ge(sDVE.h, cum_done)
                sStC.incd(sy.dma_start(out=cnt_cin.rearrange("o e -> e o"),
                                       in_=carr[0:8, NSUB:NSUB + 1]))

            cnt_store = sStC.n

            @block.gpsimd
            def _(g):
                g.wait_ge(sStC.h, cnt_store)
                ccCnt.inc(g.collective_compute(
                    "AllGather", ALU.bypass, replica_groups=RG,
                    ins=[cnt_cin.opt()], outs=[cnt_all.opt()]))
                g.wait_ge(ccCnt.h, 1)
                sCnL.incd(g.dma_start(out=cnt_sb[:, :], in_=cnt_all[:, :]))

            cnt_load = sCnL.n

            @block.tensor
            def _(tt):
                tt.wait_ge(sCnL.h, cnt_load)
                twait(tt, "ps4")
                _li = tt.matmul(ps4[0:1, 0:8], pp[0:8, PP_PREF:PP_PREF + 1],
                          cnt_sb[:, :], start=True, stop=True)
                sPE.inc(_li)

            off_pe = sPE.n

            @block.vector
            def _(v):
                sPE.wait(v, off_pe)
                _li = v.tensor_copy(orow8[:], ps4[0:1, 0:8])
                sDVE.inc(_li)

            off_dve = sDVE.n

            @block.tensor
            def _(tt):
                sDVE.wait(tt, off_dve)
                _li = tt.matmul(ps4[0:8, 8:9], orow8[:], one_f[:], start=True, stop=True)
                sPE.inc(_li)

            oc_pe = sPE.n

            @block.vector
            def _(v):
                sPE.wait(v, oc_pe)
                _li = v.tensor_scalar(basec[:], carr[0:8, 0:NSUB], ps4[0:8, 8:9],
                                None, op0=ALU.add)
                sDVE.inc(_li)

            base_dve = sDVE.n
            T["ps4"] = (sDVE, base_dve)

            @block.sync
            def _(sy):
                sy.wait_ge(sDVE.h, base_dve)
                sStC.incd(sy.dma_start(
                    out=basec_d.rearrange("o (s e) -> e (s o)", e=E),
                    in_=basec[:], allow_slow_non_contiguous=True))

            basec_st = sStC.n

            @block.sync
            def _(sy):
                sy.wait_ge(sStC.h, basec_st)
                sBr.incd(sy.dma_start(out=base_row[:], in_=basec_d[:]))

            baser_ld = sBr.n

            @block.tensor
            def _(tt):
                tt.wait_ge(sBr.h, baser_ld)
                twait(tt, "ps2")
                _li = tt.matmul(ps2[:, 0:NSUB * E], ones_row_f[:], base_row[:],
                          start=True, stop=True)
                sPE.inc(_li)

            bb_pe = sPE.n

            @block.vector
            def _(v):
                sPE.wait(v, bb_pe)
                _li = v.tensor_copy(base_bc[:, 0:NSUB * E], ps2[:, 0:NSUB * E])
                sDVE.inc(_li)

            bb_dve = sDVE.n
            T["ps2"] = (sDVE, bb_dve)

            @block.vector
            def _(v):
                sDVE.wait(v, bb_dve)
                sDVE.wait(v, lp_dve)
                vedge(v, v.tensor_tensor(scr[:], lpos_blk[:], base_bc[:], op=ALU.add))
                vedge(v, v.tensor_tensor(scr[:], scr[:], oh_s[:], op=ALU.mult))
                s3v = scr[:].rearrange("p (s e) -> p s e", e=E)
                vedge(v, v.tensor_reduce(pos_s[:].rearrange("p (s o) -> p s o", o=1), s3v,
                                         axis=AX.X, op=ALU.add))
                keep = lpos_blk[:, 0:NSUB]
                vedge(v, v.tensor_scalar(keep, pos_s[:], float(CAP), None, op0=ALU.is_lt))
                vedge(v, v.tensor_tensor(gp_s[:], grec[:], keep, op=ALU.mult))
                dstf = lpos_blk[:, NSUB:2 * NSUB]
                vedge(v, v.tensor_scalar(dstf, eidx[:], float(CAP), None, op0=ALU.mult))
                vedge(v, v.tensor_tensor(dstf, dstf, pos_s[:], op=ALU.add))
                pen = lpos_blk[:, 2 * NSUB:3 * NSUB]
                vedge(v, v.tensor_scalar(pen, keep, -BIG, BIG, op0=ALU.mult, op1=ALU.add))
                vedge(v, v.tensor_tensor(dstf, dstf, pen, op=ALU.add))
                vedge(v, v.tensor_copy(dst_u[:], dstf))
                cf = lpos_blk[:, 3 * NSUB:4 * NSUB]
                ctmp = lpos_blk[:, 5 * NSUB:6 * NSUB]
                vedge(v, v.tensor_scalar(cf, pos_s[:], float(CHK_E), None, op0=ALU.is_ge))
                for kk in range(2, NCHK_E):
                    vedge(v, v.tensor_scalar(ctmp, pos_s[:], float(kk * CHK_E), None, op0=ALU.is_ge))
                    vedge(v, v.tensor_tensor(cf, cf, ctmp, op=ALU.add))
                d2f = lpos_blk[:, 4 * NSUB:5 * NSUB]
                vedge(v, v.tensor_scalar(d2f, eidx[:], float(CHK_E), None, op0=ALU.mult))
                vedge(v, v.tensor_tensor(d2f, d2f, pos_s[:], op=ALU.add))
                vedge(v, v.tensor_scalar(cf, cf, float((E - 1) * CHK_E), None, op0=ALU.mult))
                vedge(v, v.tensor_tensor(d2f, d2f, cf, op=ALU.add))
                vedge(v, v.tensor_tensor(d2f, d2f, pen, op=ALU.add))
                _li = v.tensor_copy(dst2_u[:], d2f)
                sDVE.inc(_li)

            route_done = sDVE.n

            @block.gpsimd
            def _(g):
                g.wait_ge(sDVE.h, route_done)
                g.wait_ge(sStZ.h, src_zero_st)
                g.wait_ge(sPP.h, p0_ls)
                for s in range(NSUB):
                    sSc.incd(g.indirect_dma_start(
                        out=src_cin[:, :], out_offset=IOoA(ap=dst_u[:, s:s + 1], axis=0),
                        in_=gidx_all[:, s:s + 1], in_offset=None,
                        bounds_check=bcreg[0], oob_is_err=False))
                g.wait_ge(sSc.h, sSc.n)
                ccSrc.inc(g.collective_compute(
                    "AllReduce", ALU.add, replica_groups=RG,
                    ins=[src_cin.opt()], outs=[src_all.opt()]))

        p12.close()

        with ExitStack() as fctx:
            sbf = lambda n_, s_, d_: fctx.enter_context(nc.sbuf_tensor(n_, s_, d_))
            stgF = sbf("stgF", [128, 2 * 2048], F32)
            w2_sb = sbf("w2_sb", [128, NFC * C], BF16)
            w1_sb = sbf("w1_sb", [128, 2 * NCH * 1024], BF16)
            srcb = sbf("srcb", [CAPB, 128], F32)
            src_f = sbf("src_f", [128, CAPB], F32)
            src_u = sbf("src_u", [128, CAPB], U32)
            cslot_s = sbf("cslot_s", [CAPB, 1], U32)
            g_tm = sbf("g_tm", [128, 4 * C], BF16)
            g_fm = sbf("g_fm", [128, NCH * ST], BF16)
            h_bf = sbf("h_bf", [128, NFC * ST], BF16)
            y_bf = sbf("y_bf", [128, NCH * ST], BF16)
            ytm = sbf("ytm", [128, 4 * C], BF16)

            castF = StagedCaster(stgF, half=2048)
            fenceF = ((sStX, pass2_x2st), (sStL, pass2_lnst))
            w1_load_n = {}

            def emit_w1_load(gi, pe_wait):
                fr = gi % 4
                srcw1 = w1e.rearrange("(c p) f -> p c f", p=128)[:, :, fr * 1024:(fr + 1) * 1024]
                cw = ((sPE, pe_wait),) if pe_wait else ()
                swl = fenceF if not pe_wait else ()
                for r in range(4):
                    n = castF.emit(
                        srcw1[:, r * 2:(r + 1) * 2, :],
                        w1_sb[:, (gi % 2) * NCH * 1024 + r * 2048:
                              (gi % 2) * NCH * 1024 + (r + 1) * 2048], 2,
                        sync_waits=swl, cast_waits=cw)
                w1_load_n[gi] = n

            emit_w1_load(0, 0)
            emit_w1_load(1, 0)

            srcw2 = w2e.rearrange("(q p) co -> p q co", p=128)
            for i in range(16):
                w2_ready = castF.emit(
                    srcw2[:, i * 2:(i + 1) * 2, :],
                    w2_sb[:, i * 2 * C:(i + 1) * 2 * C], 2,
                    sync_waits=(fenceF if i < 2 else ()))

            @block.sync
            def _(sy):
                sy.wait_ge(sStX.h, pass2_x2st)
                sy.wait_ge(sStL.h, pass2_lnst)
                sCsl.incd(sy.dma_start(out=cslot_s[:], in_=cslotu_d[:]))

            cslot_ld = sCsl.n

            @block.gpsimd
            def _(g):
                g.wait_ge(ccSrc.h, 1)
                g.wait_ge(sCsl.h, cslot_ld)
                g.wait_ge(sStX.h, pass2_x2st)
                g.wait_ge(sStL.h, pass2_lnst)
                sv = src_all.rearrange("(b p) o -> b (p o)", p=128)
                sSc.incd(g.indirect_dma_start(
                    out=srcb[0:CAPB, :], out_offset=None,
                    in_=sv, in_offset=IOoA(ap=cslot_s[0:CAPB, 0:1], axis=0),
                    bounds_check=bcreg[2], oob_is_err=False))

            src_ld = sSc.n

            @block.tensor
            def _(tt):
                tt.wait_ge(sSc.h, src_ld)
                twait(tt, "ps1")
                _li = tt.matmul(ps1[0:128, 0:CAPB], srcb[0:CAPB, :],
                          consts[0:CAPB, 0:CAPB], start=True, stop=True)
                sPE.inc(_li)

            srcT_pe = sPE.n

            @block.vector
            def _(v):
                sPE.wait(v, srcT_pe)
                vedge(v, v.tensor_copy(src_f[:], ps1[:, 0:CAPB]))
                _li = v.tensor_copy(src_u[:], src_f[:])
                sDVE.inc(_li)

            src_done = sDVE.n
            T["ps1"] = (sDVE, src_done)

            Tf = {
                "gtm0": (sPE, 0), "gtm1": (sPE, 0), "gtm2": (sPE, 0), "gtm3": (sPE, 0),
                "gfm": (sPE, 0), "hbf": (sPE, 0),
                "ybf0": (sDVE, 0), "ybf1": (sDVE, 0),
                "ytm": (sStE, 0),
            }

            def tfwait(eng, *keys):
                for k in keys:
                    s, v = Tf[k]
                    eng.wait_ge(s.h, v)

            for st in range(NST):
                for s4 in range(4):
                    a = st * 4 + s4
                    gp4 = a % 4

                    @block.gpsimd
                    def _(g, a=a, gp4=gp4):
                        if a == 0:
                            g.wait_ge(sDVE.h, src_done)
                            for q in range(NCHK_T):
                                g.wait_ge(ccTok[q].h, 1)
                            g.wait_ge(sStX.h, pass2_x2st)
                        sm, vl = Tf[f"gtm{gp4}"]
                        g.wait_ge(sm.h, vl)
                        sGt[gp4].incd(g.indirect_dma_start(
                            out=g_tm[:, gp4 * C:(gp4 + 1) * C], out_offset=None,
                            in_=tok_all[:, :], in_offset=IOoA(ap=src_u[:, a:a + 1], axis=0),
                            bounds_check=bcreg[1], oob_is_err=False))

                    g_load = (sGt[gp4], sGt[gp4].n)

                    @block.tensor
                    def _(tt, s4=s4, gp4=gp4, g_load=g_load):
                        gs_, gv_ = g_load
                        tt.wait_ge(gs_.h, gv_)
                        twait(tt, "pstr")
                        if s4 == 0:
                            tfwait(tt, "gfm")
                        for c in range(NCH):
                            _li = tt.transpose(pstr[:, c * 128:(c + 1) * 128],
                                         g_tm[:, gp4 * C + c * 128:gp4 * C + (c + 1) * 128],
                                         idn_bf[:])
                        sPE.inc(_li)

                    Tf[f"gtm{gp4}"] = (sPE, sPE.n)
                    gtr_pe = sPE.n

                    @block.vector
                    def _(v, s4=s4, gtr_pe=gtr_pe):
                        sPE.wait(v, gtr_pe)
                        for c in range(NCH):
                            _li = v.tensor_copy(g_fm[:, c * ST + s4 * 128:c * ST + (s4 + 1) * 128],
                                          pstr[:, c * 128:(c + 1) * 128])
                        sDVE.inc(_li)

                    T["pstr"] = (sDVE, sDVE.n)
                gfm_ready = sDVE.n

                for fr in range(4):
                    gi = st * 4 + fr
                    buf = gi % 2
                    for fc in range(NCH):
                        fg = fr * NCH + fc
                        par = fg % 2
                        bank = ps1 if par == 0 else ps2
                        bkey = "ps1" if par == 0 else "ps2"

                        @block.tensor
                        def _(tt, fr=fr, fc=fc, fg=fg, bank=bank, bkey=bkey,
                              gi=gi, buf=buf, gfm_ready=gfm_ready):
                            if fc == 0:
                                ws, wv_ = w1_load_n[gi]
                                tt.wait_ge(ws.h, wv_)
                                if fr == 0:
                                    sDVE.wait(tt, gfm_ready)
                            twait(tt, bkey)
                            if fg == 0:
                                tfwait(tt, "hbf")
                            for ci in range(NCH):
                                _li = tt.matmul(bank[:, 0:ST],
                                          w1_sb[:, buf * NCH * 1024 + ci * 1024 + fc * 128:
                                                buf * NCH * 1024 + ci * 1024 + (fc + 1) * 128],
                                          g_fm[:, ci * ST:(ci + 1) * ST],
                                          start=(ci == 0), stop=(ci == NCH - 1))
                            sPE.inc(_li)

                        h_pe = sPE.n

                        @block.scalar
                        def _(s, fg=fg, bank=bank, h_pe=h_pe):
                            sPE.wait(s, h_pe)
                            _li = s.activation(h_bf[:, fg * ST:(fg + 1) * ST],
                                         bank[:, 0:ST],
                                         AF.Relu, bias=pp[:, PP_B1 + fg:PP_B1 + fg + 1],
                                         scale=1.0)
                            sACT.inc(_li)

                        T[bkey] = (sACT, sACT.n)

                    if gi + 2 < NST * 4:
                        emit_w1_load(gi + 2, sPE.n)

                h_done_act = sACT.n
                Tf["gfm"] = (sPE, sPE.n)

                for co in range(NCH):
                    par = co % 2
                    bank = ps3 if par == 0 else ps4
                    bkey = "ps3" if par == 0 else "ps4"

                    @block.tensor
                    def _(tt, co=co, bank=bank, bkey=bkey, h_done_act=h_done_act):
                        if co == 0:
                            sACT.wait(tt, h_done_act)
                            tt.wait_ge(w2_ready[0].h, w2_ready[1])
                        twait(tt, bkey)
                        for fq in range(NFC):
                            _li = tt.matmul(bank[:, 0:ST],
                                      w2_sb[:, fq * C + co * 128:fq * C + (co + 1) * 128],
                                      h_bf[:, fq * ST:(fq + 1) * ST],
                                      start=(fq == 0), stop=(fq == NFC - 1))
                        sPE.inc(_li)

                    y_pe = sPE.n
                    if co == NCH - 1:
                        Tf["hbf"] = (sPE, y_pe)

                    @block.vector
                    def _(v, co=co, bank=bank, y_pe=y_pe):
                        sPE.wait(v, y_pe)
                        tfwait(v, f"ybf{co % 2}")
                        _li = v.tensor_scalar(y_bf[:, co * ST:(co + 1) * ST],
                                        bank[:, 0:ST],
                                        pp[:, PP_B2 + co:PP_B2 + co + 1], None, op0=ALU.add)
                        sDVE.inc(_li)

                    Tf[f"ybf{co % 2}"] = (sDVE, sDVE.n)
                    T[bkey] = (sDVE, sDVE.n)
                y_done = sDVE.n

                ytr_dve = {}
                for co in range(NCH):
                    @block.tensor
                    def _(tt, co=co, y_done=y_done):
                        if co == 0:
                            sDVE.wait(tt, y_done)
                            tfwait(tt, "ytm")
                        twait(tt, "pstr")
                        for s4 in range(4):
                            slot = (co % 2) * 4 + s4
                            _li = tt.transpose(pstr[:, slot * 128:(slot + 1) * 128],
                                         y_bf[:, co * ST + s4 * 128:co * ST + (s4 + 1) * 128],
                                         idn_bf[:])
                        sPE.inc(_li)

                    ytr_pe = sPE.n
                    if co == NCH - 1:
                        for par in range(2):
                            Tf[f"ybf{par}"] = (sPE, ytr_pe)

                    @block.vector
                    def _(v, co=co, ytr_pe=ytr_pe):
                        sPE.wait(v, ytr_pe)
                        for s4 in range(4):
                            slot = (co % 2) * 4 + s4
                            _li = v.tensor_copy(ytm[:, s4 * C + co * 128:s4 * C + (co + 1) * 128],
                                          pstr[:, slot * 128:(slot + 1) * 128])
                        sDVE.inc(_li)

                    ytr_dve[co] = sDVE.n
                    T["pstr"] = (sDVE, sDVE.n)

                @block.scalar
                def _(sc, st=st):
                    sc.wait_ge(sDVE.h, ytr_dve[NCH - 1])
                    for s4 in range(4):
                        row0 = st * ST + s4 * 128
                        sStE.incd(sc.dma_start(out=eout_cin[row0:row0 + 128, :],
                                               in_=ytm[:, s4 * C:(s4 + 1) * C]))

                Tf["ytm"] = (sStE, sStE.n)

                spc = CHK_E // ST
                if (st + 1) % spc == 0:
                    ck = (st + 1) // spc - 1

                    @block.gpsimd
                    def _(g, ck=ck, ev=sStE.n):
                        g.wait_ge(sStE.h, ev)
                        ccEout[ck].inc(g.collective_compute(
                            "AllGather", ALU.bypass, replica_groups=RG,
                            ins=[eout_cin[ck * CHK_E:(ck + 1) * CHK_E, :].opt()],
                            outs=[eout_all[ck * E * CHK_E:(ck + 1) * E * CHK_E, :].opt()]))

            ffn_st = sStE.n

        RING = 8
        with ExitStack() as hctx:
            erow = hctx.enter_context(nc.sbuf_tensor("erow", [128, RING * C], BF16))
            x2row = hctx.enter_context(nc.sbuf_tensor("x2row", [128, RING * C], BF16))
            orow = hctx.enter_context(nc.sbuf_tensor("orow", [128, RING * C], F32))

            @block.vector
            def _(v):
                v.wait_ge(sStE.h, ffn_st)
                sDVE.inc(v.memset(erow[:], 0.0))

            erow_init = sDVE.n
            comb_dve = {}
            e_load = {}
            x_load = {}
            o_store = {}
            for s in range(NSUB):
                par = s % RING

                @block.gpsimd
                def _(g, s=s, par=par):
                    if s == 0:
                        for q in range(NCHK_E):
                            g.wait_ge(ccEout[q].h, 1)
                        g.wait_ge(sDVE.h, erow_init)
                    if s >= RING:
                        g.wait_ge(sDVE.h, comb_dve[s - RING])
                    sEr[par].incd(g.indirect_dma_start(
                        out=erow[:, par * C:(par + 1) * C], out_offset=None,
                        in_=eout_all[:, :], in_offset=IOoA(ap=dst2_u[:, s:s + 1], axis=0),
                        bounds_check=bcreg[0], oob_is_err=False))

                e_load[s] = (sEr[par], sEr[par].n)

                @block.sync
                def _(sy, s=s, par=par):
                    if s >= RING:
                        sy.wait_ge(sDVE.h, comb_dve[s - RING])
                    else:
                        sy.wait_ge(sStE.h, ffn_st)
                        sy.wait_ge(sStX.h, pass2_x2st)
                    row0 = s * 128
                    sX2r[par].incd(sy.dma_start(out=x2row[:, par * C:(par + 1) * C],
                                                in_=x2_dram[row0:row0 + 128, :]))

                x_load[s] = (sX2r[par], sX2r[par].n)

                @block.vector
                def _(v, s=s, par=par):
                    es_, ev_ = e_load[s]
                    v.wait_ge(es_.h, ev_)
                    xs_, xv_ = x_load[s]
                    v.wait_ge(xs_.h, xv_)
                    if s >= RING:
                        os_, ov_ = o_store[s - RING]
                        v.wait_ge(os_.h, ov_)
                    _li = v.scalar_tensor_tensor(
                        orow[:, par * C:(par + 1) * C],
                        erow[:, par * C:(par + 1) * C],
                        gp_s[:, s:s + 1],
                        x2row[:, par * C:(par + 1) * C],
                        op0=ALU.mult, op1=ALU.add)
                    sDVE.inc(_li)

                comb_dve[s] = sDVE.n

                @block.scalar
                def _(sc, s=s, par=par):
                    sc.wait_ge(sDVE.h, comb_dve[s])
                    row0 = s * 128
                    sOutP[par].incd(sc.dma_start(out=out_d[row0:row0 + 128, :],
                                                 in_=orow[:, par * C:(par + 1) * C]))

                o_store[s] = (sOutP[par], sOutP[par].n)

            @block.sync
            def _(sy):
                for q in range(RING):
                    sOutP[q].wait(sy)

            @block.scalar
            def _(sc):
                for q in range(RING):
                    sOutP[q].wait(sc)

            @block.gpsimd
            def _(g):
                for q in range(RING):
                    sOutP[q].wait(g)

    return nc


def _host_prep(inputs, N=4096, CAP=5120):
    CAPB = CAP // 128
    CHK_T = 1024 if N % 1024 == 0 else N
    x = np.asarray(inputs["L0_attn_x"], np.float32)
    Wq = np.asarray(inputs["L0_attn_Wq"], np.float32)
    Wk = np.asarray(inputs["L0_attn_Wk"], np.float32)
    Wv = np.asarray(inputs["L0_attn_Wv"], np.float32)
    Wo = np.asarray(inputs["L0_attn_Wo"], np.float32)
    bq = np.asarray(inputs["L0_attn_bq"], np.float32)
    bk = np.asarray(inputs["L0_attn_bk"], np.float32)
    bv = np.asarray(inputs["L0_attn_bv"], np.float32)
    bo = np.asarray(inputs["L0_attn_bo"], np.float32)
    ln_g = np.asarray(inputs["L0_attn_ln_g"], np.float32)
    ln_b = np.asarray(inputs["L0_attn_ln_b"], np.float32)
    wg = np.asarray(inputs["L0_ffn_wg"], np.float32)
    w1 = np.asarray(inputs["L0_ffn_w1"], np.float32)
    b1 = np.asarray(inputs["L0_ffn_b1"], np.float32)
    w2 = np.asarray(inputs["L0_ffn_w2"], np.float32)
    b2 = np.asarray(inputs["L0_ffn_b2"], np.float32)

    wqT = np.ascontiguousarray(Wq.T)
    wkT = np.ascontiguousarray(Wk.T)
    wvT = np.ascontiguousarray(Wv.T)
    woT = np.ascontiguousarray(Wo.T)

    consts = np.zeros((128, 256), np.float32)
    consts[:, 0:128] = np.eye(128, dtype=np.float32)
    k_idx = np.arange(128)[:, None]
    i_idx = np.arange(128)[None, :]
    consts[:, 128:256] = (k_idx < i_idx).astype(np.float32)

    rows = np.zeros((3, C), np.float32)
    rows[0] = bq
    rows[1] = bv
    rows[2] = bo

    in_maps = []
    for core in range(NCORES):
        pp = np.zeros((128, PP_COLS), np.float32)
        pp[:, PP_BK:PP_BK + 8] = bk.reshape(8, 128).T
        pp[:, PP_BO:PP_BO + 8] = bo.reshape(8, 128).T
        pp[:, PP_LNG:PP_LNG + 8] = ln_g.reshape(8, 128).T
        pp[:, PP_LNB:PP_LNB + 8] = ln_b.reshape(8, 128).T
        pp[:, PP_WG:PP_WG + 64] = wg.reshape(8, 128, 8).transpose(1, 0, 2).reshape(128, 64)
        pp[:, PP_B1:PP_B1 + 32] = b1[core].reshape(32, 128).T
        pp[:, PP_B2:PP_B2 + 8] = b2[core].reshape(8, 128).T
        pp[0:8, PP_PREF] = (np.arange(8) < core).astype(np.float32)
        pp[:, PP_CBASE] = float(core * N)
        in_maps.append(dict(
            xT=np.ascontiguousarray(x[core].T),
            wqT=wqT, wkT=wkT, wvT=wvT, woT=woT,
            w1e=np.ascontiguousarray(w1[core]),
            w2e=np.ascontiguousarray(w2[core]),
            pp_d=pp, rows_d=rows, consts_d=consts,
            cslotu=(core * CAPB + np.arange(CAPB, dtype=np.uint32))[:, None],
            gidx_d=np.ascontiguousarray(
                ((np.arange(N) // CHK_T) * (NCORES * CHK_T) + core * CHK_T
                 + (np.arange(N) % CHK_T)).astype(np.float32)
                .reshape(N // 128, 128).T),
        ))
    return in_maps


def kernel(**inputs):
    in_maps = _host_prep(inputs)
    nc = build_program()
    res = run_bass_kernel_spmd(nc, in_maps, core_ids=list(range(NCORES)))
    out = np.stack([np.asarray(res.results[i]["out"]) for i in range(NCORES)], axis=0)
    return out.astype(np.float32)


# version 23
# speedup vs baseline: 1.0310x; 1.0310x over previous
import numpy as np
from contextlib import ExitStack

import concourse.bass as bass
import concourse.mybir as mybir
from concourse.bass_utils import run_bass_kernel_spmd

F32 = mybir.dt.float32
BF16 = mybir.dt.bfloat16
U32 = mybir.dt.uint32
AF = mybir.ActivationFunctionType
ALU = mybir.AluOpType
AX = mybir.AxisListType
IOoA = bass.IndirectOffsetOnAxis

NCORES = 8
C = 1024
NCH = 8
H = 16
D = 64
E = 8
DFF = 4096
NFC = DFF // 128
EPS = 1e-5
TF = 512
ST = 512
BIG = 8_000_000.0

PP_BK = 0
PP_BO = 8
PP_LNG = 16
PP_LNB = 24
PP_WG = 32
PP_B1 = 96
PP_B2 = 128
PP_PREF = 136
PP_CBASE = 137
PP_COLS = 144


class Sem:
    def __init__(self, ctx, nc, name):
        self.h = ctx.enter_context(nc.semaphore(name))
        self.n = 0

    def inc(self, instr, k=1):
        instr.then_inc(self.h, k)
        self.n += k
        return instr

    def incd(self, instr):
        return self.inc(instr, 16)

    def wait(self, eng, val=None):
        eng.wait_ge(self.h, self.n if val is None else val)


def build_program(N=4096, CAP=5120):
    NT = N // TF
    NSUB = N // 128
    NST = CAP // ST
    CAPB = CAP // 128
    CHK_T = 1024 if N % 1024 == 0 else N
    NCHK_T = N // CHK_T
    CHK_E = 1024 if CAP % 1024 == 0 else CAP
    NCHK_E = CAP // CHK_E

    nc = bass.Bass("TRN2", target_bir_lowering=False, debug=False)

    xT = nc.dram_tensor("xT", [C, N], F32, kind="ExternalInput").ap()
    wqT = nc.dram_tensor("wqT", [C, C], F32, kind="ExternalInput").ap()
    wkT = nc.dram_tensor("wkT", [C, C], F32, kind="ExternalInput").ap()
    wvT = nc.dram_tensor("wvT", [C, C], F32, kind="ExternalInput").ap()
    woT = nc.dram_tensor("woT", [C, C], F32, kind="ExternalInput").ap()
    w1e = nc.dram_tensor("w1e", [C, DFF], F32, kind="ExternalInput").ap()
    w2e = nc.dram_tensor("w2e", [DFF, C], F32, kind="ExternalInput").ap()
    pp_d = nc.dram_tensor("pp_d", [128, PP_COLS], F32, kind="ExternalInput").ap()
    rows_d = nc.dram_tensor("rows_d", [3, C], F32, kind="ExternalInput").ap()
    consts_d = nc.dram_tensor("consts_d", [128, 256], F32, kind="ExternalInput").ap()

    out_d = nc.dram_tensor("out", [N, C], F32, kind="ExternalOutput").ap()

    x2_dram = nc.dram_tensor("x2_dram", [N, C], BF16).ap()
    ln2_cin = nc.dram_tensor("ln2_cin", [N, C], BF16).ap()
    tok_all = nc.dram_tensor("tok_all", [NCORES * N, C], BF16).ap()
    cnt_cin = nc.dram_tensor("cnt_cin", [1, E], F32).ap()
    cnt_all = nc.dram_tensor("cnt_all", [NCORES, E], F32).ap()
    src_cin = nc.dram_tensor("src_cin", [E * CAP, 1], F32).ap()
    src_all = nc.dram_tensor("src_all", [E * CAP, 1], F32).ap()
    cslotu_d = nc.dram_tensor("cslotu", [CAPB, 1], U32, kind="ExternalInput").ap()
    gidx_d = nc.dram_tensor("gidx_d", [128, N // 128], F32, kind="ExternalInput").ap()
    eout_cin = nc.dram_tensor("eout_cin", [CAP, C], BF16).ap()
    eout_all = nc.dram_tensor("eout_all", [E * CAP, C], BF16).ap()
    basec_d = nc.dram_tensor("basec_d", [1, (N // 128) * E], F32).ap()

    RG = [list(range(NCORES))]

    with ExitStack() as ctx:
        sbg = lambda name, shape, dt: ctx.enter_context(nc.sbuf_tensor(name, shape, dt))

        pp = sbg("pp", [128, PP_COLS], F32)
        consts = sbg("consts", [128, 256], F32)
        idn_bf = sbg("idn_bf", [128, 128], BF16)
        ones_col_bf = sbg("ones_col_bf", [128, 1], BF16)
        ones_row_bf = sbg("ones_row_bf", [1, 128], BF16)
        ones_t = sbg("ones_t", [65, 128], F32)
        ones_row_f = ones_t[0:1, :]
        ones_row_f64 = ones_t[64:65, :]
        ones_col_f = sbg("ones_col_f", [128, 1], F32)
        one_f = sbg("one_f", [1, 1], F32)
        eps_t = sbg("eps_t", [65, 1], F32)
        eps1 = eps_t[0:1, :]
        eps1_64 = eps_t[64:65, :]
        bq_row = sbg("bq_row", [1, C], BF16)
        bv_row = sbg("bv_row", [1, C], BF16)
        zero_f = sbg("zero_f", [128, 128], F32)
        zrow_bf = sbg("zrow_bf", [1, 512], BF16)
        iota8_row = sbg("iota8_row", [128, E], F32)
        gidx_all = sbg("gidx_all", [128, NSUB], F32)
        glog = sbg("glog", [128, NSUB * E], F32)
        grec = sbg("grec", [128, NSUB], F32)
        eidx = sbg("eidx", [128, NSUB], F32)
        oh_s = sbg("oh_s", [128, NSUB * E], F32)
        lpos_blk = sbg("lpos_blk", [128, NSUB * E], F32)
        base_bc = sbg("base_bc", [128, NSUB * E], F32)
        pos_s = sbg("pos_s", [128, NSUB], F32)
        gp_s = sbg("gp_s", [128, NSUB], F32)
        dst_u = sbg("dst_u", [128, NSUB], U32)
        dst2_u = sbg("dst2_u", [128, NSUB], U32)
        chk_u = sbg("chk_u", [128, NSUB], U32)
        scr = sbg("scr", [128, NSUB * E], F32)
        carr = sbg("carr", [8, NSUB + 1], F32)
        basec = sbg("basec", [8, NSUB], F32)
        base_row = sbg("base_row", [1, NSUB * E], F32)
        cnt_sb = sbg("cnt_sb", [8, E], F32)
        orow8 = sbg("orow8", [1, E], F32)

        p12 = ExitStack()
        sb12 = lambda name, shape, dt: p12.enter_context(nc.sbuf_tensor(name, shape, dt))
        strow_t = sb12("strow_t", [65, 3 * TF], F32)
        strow = strow_t[0:1, :]
        strow2 = strow_t[64:65, :]
        sqbf = sb12("sqbf", [128, NCH * TF], BF16)
        cbf = sb12("cbf", [128, NCH * TF], BF16)
        qs_fm = sb12("qs_fm", [128, NCH * N], BF16)
        s_bf = sb12("s_bf", [128, H * D // 2], BF16)
        z_sb = sb12("z_sb", [128, 16], F32)

        sPP = Sem(ctx, nc, "sPP")
        sXa = [Sem(ctx, nc, "sXa0"), Sem(ctx, nc, "sXa1")]
        sX2r = [Sem(ctx, nc, f"sX2r{q}") for q in range(8)]
        sBr = Sem(ctx, nc, "sBr")
        sBo = Sem(ctx, nc, "sBo")
        sCsl = Sem(ctx, nc, "sCsl")
        sStZ = Sem(ctx, nc, "sStZ")
        sStX = Sem(ctx, nc, "sStX")
        sStL = Sem(ctx, nc, "sStL")
        sStC = Sem(ctx, nc, "sStC")
        sStE = Sem(ctx, nc, "sStE")
        sOutP = [Sem(ctx, nc, f"sOut{q}") for q in range(8)]
        sStg = [Sem(ctx, nc, "sStg0"), Sem(ctx, nc, "sStg1")]
        sCst = Sem(ctx, nc, "sCst")
        sWgt = Sem(ctx, nc, "sWgt")
        sXb = [Sem(ctx, nc, "sXb0"), Sem(ctx, nc, "sXb1")]
        sCnL = Sem(ctx, nc, "sCnL")
        sW2 = Sem(ctx, nc, "sW2")
        sW1 = [Sem(ctx, nc, "sW10"), Sem(ctx, nc, "sW11")]
        sSc = Sem(ctx, nc, "sSc")
        sGt = [Sem(ctx, nc, f"sGt{q}") for q in range(4)]
        sEr = [Sem(ctx, nc, f"sEr{q}") for q in range(8)]
        sPE = Sem(ctx, nc, "sPE")
        sACT = Sem(ctx, nc, "sACT")
        sDVE = Sem(ctx, nc, "sDVE")
        sGP = Sem(ctx, nc, "sGP")
        ccCnt = Sem(ctx, nc, "ccCnt")
        ccTok = [Sem(ctx, nc, f"ccTok{q}") for q in range(max(1, N // (1024 if N % 1024 == 0 else N)))]
        ccSrc = Sem(ctx, nc, "ccSrc")
        ccEout = [Sem(ctx, nc, f"ccEout{q}") for q in range(max(1, CAP // (1024 if CAP % 1024 == 0 else CAP)))]

        block = ctx.enter_context(nc.Block())

        ps1 = ctx.enter_context(nc.psum_tensor("ps1", [128, 512], F32))
        ps2 = ctx.enter_context(nc.psum_tensor("ps2", [128, 512], F32))
        ps3 = ctx.enter_context(nc.psum_tensor("ps3", [128, 512], F32))
        ps4 = ctx.enter_context(nc.psum_tensor("ps4", [128, 512], F32))
        pstr = ctx.enter_context(nc.psum_tensor("pstr", [128, 1024], BF16))
        psE = ctx.enter_context(nc.psum_tensor("psE", [128, 512], F32))
        psLnA = p12.enter_context(nc.psum_tensor("psLnA", [128, 512], F32))
        psLnB = p12.enter_context(nc.psum_tensor("psLnB", [128, 512], F32))

        T = {
            "ps1": (sDVE, 0), "ps2": (sDVE, 0), "ps3": (sDVE, 0), "ps4": (sDVE, 0),
            "pstr": (sDVE, 0), "psLnA": (sDVE, 0), "psLnB": (sDVE, 0), "psE": (sDVE, 0),
            "sqbf": (sPE, 0), "cbf": (sPE, 0), "strow": (sPE, 0),
            "pstr2": (sDVE, 0),
        }

        def twait(eng, *keys):
            for k in keys:
                s, v = T[k]
                eng.wait_ge(s.h, v)

        def vedge(v, instr):
            sDVE.inc(instr)
            sDVE.wait(v)
            return instr

        class StagedCaster:

            def __init__(self, stg, half=4096):
                self.stg = stg
                self.half = half
                self.cast_n = [0, 0]
                self.k = 0

            def emit(self, src_ap, dst_ap, nch, sync_waits=(), cast_waits=(),
                     eng="v"):
                par = self.k % 2
                self.k += 1
                sview = self.stg[:, par * self.half:(par + 1) * self.half]
                csem = sACT if eng == "s" else sDVE

                @block.sync
                def _(sy, par=par):
                    for sm, vl in sync_waits:
                        sy.wait_ge(sm.h, vl)
                    if self.cast_n[par]:
                        cs, cv = self.cast_n[par]
                        sy.wait_ge(cs.h, cv)
                    sStg[par].incd(sy.dma_start(
                        out=sview.rearrange("p (c co) -> p c co", c=nch),
                        in_=src_ap))

                ld = sStg[par].n

                if eng == "s":
                    @block.scalar
                    def _(s, par=par, ld=ld):
                        s.wait_ge(sStg[par].h, ld)
                        for sm, vl in cast_waits:
                            s.wait_ge(sm.h, vl)
                        sACT.inc(s.activation(dst_ap, sview, AF.Copy,
                                              bias=0.0, scale=1.0))
                else:
                    @block.vector
                    def _(v, par=par, ld=ld):
                        v.wait_ge(sStg[par].h, ld)
                        for sm, vl in cast_waits:
                            v.wait_ge(sm.h, vl)
                        sDVE.inc(v.tensor_copy(dst_ap, sview))

                self.cast_n[par] = (csem, csem.n)
                return (csem, csem.n)

        @block.sync
        def _(sy):
            sPP.incd(sy.dma_start(out=pp[:], in_=pp_d[:]))
            sPP.incd(sy.dma_start(out=consts[:], in_=consts_d[:]))

        p0_ls = sPP.n

        bcreg = []

        @block.gpsimd
        def _(g):
            r1 = g.alloc_register()
            g.reg_mov(r1, E * CAP - 1)
            r2 = g.alloc_register()
            g.reg_mov(r2, NCORES * N - 1)
            r3 = g.alloc_register()
            g.reg_mov(r3, E * CAPB - 1)
            bcreg.extend([r1, r2, r3])
            sCst.incd(g.dma_start(out=idn_bf[:], in_=consts_d[:, 0:128]))
            sCst.incd(g.dma_start(out=bq_row[:], in_=rows_d[0:1, :]))
            sCst.incd(g.dma_start(out=bv_row[:], in_=rows_d[1:2, :]))
            g.iota(iota8_row[:], pattern=[[1, E]], base=0, channel_multiplier=0,
                   allow_small_or_imprecise_dtypes=True)
            sGP.inc(g.iota(carr[:, 0:1], pattern=[[1, 1]], base=0, channel_multiplier=0,
                           allow_small_or_imprecise_dtypes=True))

        p0_gp = sGP.n
        p0_lg = sCst.n

        @block.vector
        def _(v):
            v.memset(ones_col_bf[:], 1.0)
            v.memset(ones_row_bf[:], 1.0)
            v.memset(ones_t[:], 1.0)
            v.memset(ones_col_f[:], 1.0)
            v.memset(one_f[:], 1.0)
            v.memset(eps_t[:], EPS)
            v.memset(zrow_bf[:], 0.0)
            sDVE.inc(v.memset(zero_f[:], 0.0))

        p0_dve = sDVE.n

        @block.sync
        def _(sy):
            sy.wait_ge(sDVE.h, p0_dve)
            sv = src_cin.rearrange("(b p) o -> b (p o)", p=128)
            nb = (E * CAP) // 128
            step = 128
            for b0 in range(0, nb, step):
                bn = min(step, nb - b0)
                sStZ.incd(sy.dma_start(out=sv[b0:b0 + bn, :],
                                      in_=zero_f[0:bn, 0:128]))

        src_zero_st = sStZ.n

        @block.sync
        def _(sy):
            sPP.incd(sy.dma_start(out=gidx_all[:], in_=gidx_d[:]))

        p0_ls = sPP.n
        gidx_done = None

        @block.tensor
        def _(t):
            t.wait_ge(sDVE.h, p0_dve)
            t.wait_ge(sPP.h, p0_ls)
            t.wait_ge(sCst.h, p0_lg)

        @block.scalar
        def _(s):
            s.wait_ge(sDVE.h, p0_dve)
            s.wait_ge(sPP.h, p0_ls)
            s.wait_ge(sCst.h, p0_lg)

        @block.vector
        def _(v):
            v.wait_ge(sPP.h, p0_ls)
            v.wait_ge(sCst.h, p0_lg)

        def LN_fm(xf, emit_out, src_bf=None, pre_sc=None, pre_dv=None):
            @block.scalar
            def _(s):
                if pre_sc:
                    for sm, vl in pre_sc:
                        s.wait_ge(sm.h, vl)
                twait(s, "sqbf")
                for c in range(NCH):
                    _li = s.activation(sqbf[:, c * TF:(c + 1) * TF], xf(c),
                                 AF.Square, bias=0.0, scale=1.0)
                sACT.inc(_li)

            a_sq = sACT.n

            if src_bf is None:
                @block.vector
                def _(v):
                    if pre_dv:
                        for sm, vl in pre_dv:
                            v.wait_ge(sm.h, vl)
                    twait(v, "cbf")
                    for c in range(NCH):
                        _li = v.tensor_copy(cbf[:, c * TF:(c + 1) * TF], xf(c))
                    sDVE.inc(_li)
                d_cb = sDVE.n
                ssrc = lambda c: cbf[:, c * TF:(c + 1) * TF]
            else:
                if pre_dv:
                    @block.vector
                    def _(v):
                        for sm, vl in pre_dv:
                            v.wait_ge(sm.h, vl)
                d_cb = None
                ssrc = src_bf

            @block.tensor
            def _(t):
                sACT.wait(t, a_sq)
                if d_cb is not None:
                    sDVE.wait(t, d_cb)
                twait(t, "psLnA", "psLnB", "strow")
                for c in range(NCH):
                    t.matmul(psLnA[0:1, 0:TF], ones_col_bf[:], ssrc(c),
                             start=(c == 0), stop=(c == NCH - 1))
                for c in range(NCH):
                    _li = t.matmul(psLnB[0:1, 0:TF], ones_col_bf[:],
                             sqbf[:, c * TF:(c + 1) * TF],
                             start=(c == 0), stop=(c == NCH - 1))
                sPE.inc(_li)

            p_sums = sPE.n
            T["sqbf"] = (sPE, p_sums)
            if src_bf is None:
                T["cbf"] = (sPE, p_sums)

            @block.vector
            def _(v):
                sPE.wait(v, p_sums)
                mu, va = strow[:, 0:TF], strow[:, TF:2 * TF]
                aa = strow[:, 2 * TF:3 * TF]
                v.tensor_scalar(mu, psLnA[0:1, 0:TF], 1.0 / C, None, op0=ALU.mult)
                vedge(v, v.tensor_scalar(va, psLnB[0:1, 0:TF], 1.0 / C, None, op0=ALU.mult))
                vedge(v, v.tensor_tensor(aa, mu, mu, op=ALU.mult))
                sDVE.inc(v.tensor_tensor(va, va, aa, op=ALU.subtract))

            d_var = sDVE.n

            @block.scalar
            def _(s):
                sDVE.wait(s, d_var)
                _li = s.activation(strow[:, 2 * TF:3 * TF], strow[:, TF:2 * TF],
                             AF.Ln, bias=eps1[:], scale=1.0)
                sACT.inc(_li)

            a_ln = sACT.n

            @block.scalar
            def _(s):
                sACT.wait(s, a_ln)
                sACT.inc(s.activation(strow[:, TF:2 * TF], strow[:, 2 * TF:3 * TF],
                                      AF.Exp, bias=0.0, scale=-0.5))

            a_rstd = sACT.n

            @block.tensor
            def _(t):
                sACT.wait(t, a_rstd)
                t.matmul(psLnA[:, 0:TF], ones_row_f[:], strow[:, 0:TF],
                         start=True, stop=True)
                _li = t.matmul(psLnB[:, 0:TF], ones_row_f[:], strow[:, TF:2 * TF],
                         start=True, stop=True)
                sPE.inc(_li)

            p_bc = sPE.n
            T["strow"] = (sPE, p_bc)

            @block.vector
            def _(v):
                sPE.wait(v, p_bc)
                last = emit_out(v, psLnA[:, 0:TF], psLnB[:, 0:TF], vedge)
                sDVE.inc(last)

            done = sDVE.n
            T["psLnA"] = (sDVE, done)
            T["psLnB"] = (sDVE, done)
            return done

        with ExitStack() as p1ctx:
            sb1 = lambda n_, s_, d_: p1ctx.enter_context(nc.sbuf_tensor(n_, s_, d_))
            xb = [sb1(f"xb{i}", [128, NCH * TF], BF16) for i in range(2)]
            ln1 = [sb1(f"ln1_{i}", [128, NCH * TF], BF16) for i in range(2)]
            wq_sb = sb1("wq_sb", [128, NCH * C], BF16)
            wk_sb = sb1("wk_sb", [128, NCH * C], BF16)
            wv_sb = sb1("wv_sb", [128, NCH * C], BF16)
            ek_fm = sb1("ek_fm", [128, 2 * TF], BF16)
            ek_tm = sb1("ek_tm", [128, 4 * C], BF16)
            v_bf = sb1("v_bf", [128, 2 * C], BF16)
            eq_s = sb1("eq_s", [128, C], F32)
            qrow = sb1("qrow", [128, 2 * H], F32)
            qs_tm = sb1("qs_tm", [128, 2 * C], BF16)
            zred = sb1("zred", [128, 8], F32)

            @block.gpsimd
            def _(g):
                for w_d, w_s in [(wqT, wq_sb), (wkT, wk_sb), (wvT, wv_sb)]:
                    srcv = w_d.rearrange("(c p) co -> p c co", p=128)
                    dstv = w_s[:].rearrange("p (c co) -> p c co", c=NCH)
                    for i in range(2):
                        sWgt.incd(g.dma_start(out=dstv[:, i * 4:(i + 1) * 4, :],
                                              in_=srcv[:, i * 4:(i + 1) * 4, :]))

            w_attn_ready = (sWgt, sWgt.n)

            xload = {}

            @block.gpsimd
            def _(g):
                for t in range(min(2, NT)):
                    src = xT.rearrange("(c p) n -> p c n", p=128)[:, :, t * TF:(t + 1) * TF]
                    dst = xb[t % 2][:].rearrange("p (c n) -> p c n", c=NCH)
                    sXb[t % 2].incd(g.dma_start(out=dst, in_=src))
                    xload[t] = (sXb[t % 2], sXb[t % 2].n)

            Tp = {
                "ln1_0": (sPE, 0), "ln1_1": (sPE, 0),
                "ekfm0": [(sPE, 0)], "ekfm1": [(sPE, 0)],
                "ektm": (sPE, 0),
                "vbf0": (sPE, 0), "vbf1": (sPE, 0),
                "eq": (sDVE, 0),
                "qstm0": (sPE, 0), "qstm1": (sPE, 0),
            }

            def tpwait(eng, *keys):
                for k in keys:
                    s, v = Tp[k]
                    eng.wait_ge(s.h, v)

            ln_a_rstd = {}

            def ln1_front(t):
                xt = xb[t % 2]

                @block.scalar
                def _(s, t=t):
                    xs, xv = xload[t]
                    s.wait_ge(xs.h, xv)
                    twait(s, "sqbf")
                    for c in range(NCH):
                        _li = s.activation(sqbf[:, c * TF:(c + 1) * TF],
                                           xt[:, c * TF:(c + 1) * TF],
                                           AF.Square, bias=0.0, scale=1.0)
                    sACT.inc(_li)

                a_sq = sACT.n

                @block.tensor
                def _(tt, t=t):
                    sACT.wait(tt, a_sq)
                    twait(tt, "psLnA", "psLnB", "strow")
                    for c in range(NCH):
                        tt.matmul(psLnA[0:1, 0:TF], ones_col_bf[:],
                                  xt[:, c * TF:(c + 1) * TF],
                                  start=(c == 0), stop=(c == NCH - 1))
                    for c in range(NCH):
                        _li = tt.matmul(psLnB[0:1, 0:TF], ones_col_bf[:],
                                        sqbf[:, c * TF:(c + 1) * TF],
                                        start=(c == 0), stop=(c == NCH - 1))
                    sPE.inc(_li)

                p_sums = sPE.n
                T["sqbf"] = (sPE, p_sums)

                @block.vector
                def _(v):
                    sPE.wait(v, p_sums)
                    mu, va = strow[:, 0:TF], strow[:, TF:2 * TF]
                    aa = strow[:, 2 * TF:3 * TF]
                    v.tensor_scalar(mu, psLnA[0:1, 0:TF], 1.0 / C, None, op0=ALU.mult)
                    vedge(v, v.tensor_scalar(va, psLnB[0:1, 0:TF], 1.0 / C, None, op0=ALU.mult))
                    vedge(v, v.tensor_tensor(aa, mu, mu, op=ALU.mult))
                    sDVE.inc(v.tensor_tensor(va, va, aa, op=ALU.subtract))

                d_var = sDVE.n

                @block.scalar
                def _(s):
                    sDVE.wait(s, d_var)
                    _li = s.activation(strow[:, 2 * TF:3 * TF], strow[:, TF:2 * TF],
                                       AF.Ln, bias=eps1[:], scale=1.0)
                    sACT.inc(_li)

                a_ln = sACT.n

                @block.scalar
                def _(s):
                    sACT.wait(s, a_ln)
                    _li = s.activation(strow[:, TF:2 * TF], strow[:, 2 * TF:3 * TF],
                                       AF.Exp, bias=0.0, scale=-0.5)
                    sACT.inc(_li)

                ln_a_rstd[t] = sACT.n

            def ln1_back(t):
                par = t % 2
                xt = xb[par]
                lnd = ln1[par]

                @block.tensor
                def _(tt, t=t):
                    sACT.wait(tt, ln_a_rstd[t])
                    tt.matmul(psLnA[:, 0:TF], ones_row_f[:], strow[:, 0:TF],
                              start=True, stop=True)
                    _li = tt.matmul(psLnB[:, 0:TF], ones_row_f[:], strow[:, TF:2 * TF],
                                    start=True, stop=True)
                    sPE.inc(_li)

                p_bc = sPE.n
                T["strow"] = (sPE, p_bc)

                @block.vector
                def _(v, t=t, par=par):
                    sPE.wait(v, p_bc)
                    sm, vl = Tp[f"ln1_{par}"]
                    v.wait_ge(sm.h, vl)
                    twait(v, "cbf")
                    for c in range(NCH):
                        ins = v.tensor_tensor(cbf[:, c * TF:(c + 1) * TF],
                                              xt[:, c * TF:(c + 1) * TF],
                                              psLnA[:, 0:TF], op=ALU.subtract)
                    vedge(v, ins)
                    for c in range(NCH):
                        ins = v.tensor_tensor(lnd[:, c * TF:(c + 1) * TF],
                                              cbf[:, c * TF:(c + 1) * TF],
                                              psLnB[:, 0:TF], op=ALU.mult)
                    sDVE.inc(ins)

                done = sDVE.n
                T["psLnA"] = (sDVE, done)
                T["psLnB"] = (sDVE, done)
                T["cbf"] = (sDVE, done)
                return done

            ln1_front(0)
            ln1_done = {0: ln1_back(0)}
            z_acc_n = 0

            for t in range(NT):
                par_t = t % 2
                lnt_bf = ln1[par_t]

                if t + 2 < NT:
                    @block.gpsimd
                    def _(g, t=t, lv=ln1_done[t]):
                        g.wait_ge(sDVE.h, lv)
                        src = xT.rearrange("(c p) n -> p c n", p=128)[:, :, (t + 2) * TF:(t + 3) * TF]
                        dst = xb[t % 2][:].rearrange("p (c n) -> p c n", c=NCH)
                        sXb[t % 2].incd(g.dma_start(out=dst, in_=src))
                        xload[t + 2] = (sXb[t % 2], sXb[t % 2].n)

                k_act = {}
                for co in range(NCH):
                    bank = ps1 if co % 2 == 0 else ps2
                    bkey = "ps1" if co % 2 == 0 else "ps2"

                    @block.tensor
                    def _(tt, co=co, bank=bank, bkey=bkey, t=t, lnt_bf=lnt_bf):
                        if co == 0:
                            tt.wait_ge(w_attn_ready[0].h, w_attn_ready[1])
                            sDVE.wait(tt, ln1_done[t])
                        twait(tt, bkey)
                        for ci in range(NCH):
                            _li = tt.matmul(bank[:, 0:TF],
                                      wk_sb[:, ci * C + co * 128:ci * C + (co + 1) * 128],
                                      lnt_bf[:, ci * TF:(ci + 1) * TF],
                                      start=(ci == 0), stop=(ci == NCH - 1))
                        sPE.inc(_li)

                    k_pe = sPE.n

                    @block.scalar
                    def _(s, co=co, bank=bank, k_pe=k_pe):
                        sPE.wait(s, k_pe)
                        for sm, vl in Tp[f"ekfm{co % 2}"]:
                            s.wait_ge(sm.h, vl)
                        sACT.inc(s.activation(
                            ek_fm[:, (co % 2) * TF:(co % 2 + 1) * TF], bank[:, 0:TF],
                            AF.Exp, bias=pp[:, PP_BK + co:PP_BK + co + 1], scale=1.0))

                    k_act[co] = sACT.n
                    T[bkey] = (sACT, sACT.n)

                    @block.tensor
                    def _(tt, co=co, t=t):
                        sACT.wait(tt, k_act[co])
                        twait(tt, "pstr")
                        if co == 0:
                            tpwait(tt, "ektm")
                        for s4 in range(4):
                            slot = (co % 2) * 4 + s4
                            _li = tt.transpose(pstr[:, slot * 128:(slot + 1) * 128],
                                         ek_fm[:, (co % 2) * TF + s4 * 128:(co % 2) * TF + (s4 + 1) * 128],
                                         idn_bf[:])
                        sPE.inc(_li)

                    k_tr = sPE.n

                    @block.vector
                    def _(v, co=co, k_tr=k_tr, z_acc_n=z_acc_n):
                        sPE.wait(v, k_tr)
                        if z_acc_n:
                            v.wait_ge(sDVE.h, z_acc_n)
                        for s4 in range(4):
                            slot = (co % 2) * 4 + s4
                            v.tensor_copy(ek_tm[:, s4 * C + co * 128:s4 * C + (co + 1) * 128],
                                          pstr[:, slot * 128:(slot + 1) * 128])
                        _li = v.tensor_reduce(
                            zred[:, co:co + 1].rearrange("p (a o) -> p a o", o=1),
                            ek_fm[:, (co % 2) * TF:(co % 2 + 1) * TF]
                            .rearrange("p (a n) -> p a n", a=1),
                            axis=AX.X, op=ALU.add)
                        sDVE.inc(_li)

                    Tp[f"ekfm{co % 2}"] = [(sPE, k_tr), (sDVE, sDVE.n)]
                    T["pstr"] = (sDVE, sDVE.n)
                ektm_ready = sDVE.n

                @block.vector
                def _(v, t=t, ektm_ready=ektm_ready):
                    sDVE.wait(v, ektm_ready)
                    if t == 0:
                        _li = v.tensor_copy(z_sb[:, 0:8], zred[:])
                    else:
                        _li = v.tensor_tensor(z_sb[:, 0:8], z_sb[:, 0:8], zred[:],
                                              op=ALU.add)
                    sDVE.inc(_li)

                z_acc_n = sDVE.n

                if t + 1 < NT:
                    ln1_front(t + 1)

                def emit_qs_tr_state(s4, qs_done, v_evac, t=t):
                    gs = t * 4 + s4
                    par = s4 % 2

                    @block.tensor
                    def _(tt, s4=s4, qs_done=qs_done):
                        sDVE.wait(tt, qs_done)
                        twait(tt, "pstr")
                        for c in range(NCH):
                            _li = tt.transpose(pstr[:, c * 128:(c + 1) * 128],
                                         qs_tm[:, par * C + c * 128:par * C + (c + 1) * 128],
                                         idn_bf[:])
                        sPE.inc(_li)

                    qtr_pe = sPE.n
                    Tp[f"qstm{par}"] = (sPE, qtr_pe)

                    @block.vector
                    def _(v, gs=gs, qtr_pe=qtr_pe):
                        sPE.wait(v, qtr_pe)
                        for c in range(NCH):
                            _li = v.tensor_copy(qs_fm[:, c * N + gs * 128:c * N + (gs + 1) * 128],
                                          pstr[:, c * 128:(c + 1) * 128])
                        sDVE.inc(_li)

                    T["pstr"] = (sDVE, sDVE.n)

                    @block.tensor
                    def _(tt, s4=s4, gs=gs, par=par, v_evac=v_evac):
                        sDVE.wait(tt, v_evac)
                        if s4 == 0:
                            sDVE.wait(tt, ektm_ready)
                        first, last = (gs == 0), (gs == NSUB - 1)
                        if first:
                            tt.matmul(psE[:, 0:512], ones_row_bf[:], zrow_bf[:],
                                      start=True, stop=False)
                        for h in range(H):
                            hb = (h % 2) * 64
                            hc = (h // 2) * D
                            _li = tt.matmul(psE[hb:hb + 64, hc:hc + D],
                                      ek_tm[:, s4 * C + h * D:s4 * C + (h + 1) * D],
                                      v_bf[:, par * C + h * D:par * C + (h + 1) * D],
                                      start=False, stop=False)
                        if last:
                            _li = tt.matmul(psE[:, 0:512], ones_row_bf[:], zrow_bf[:],
                                            start=False, stop=True)
                        sPE.inc(_li)

                    Tp[f"vbf{par}"] = (sPE, sPE.n)
                    Tp["ektm"] = (sPE, sPE.n)

                pend = None
                for s4 in range(4):
                    par = s4 % 2

                    @block.tensor
                    def _(tt, s4=s4, par=par, lnt_bf=lnt_bf):
                        twait(tt, "ps3", "ps4")
                        sm, vl = Tp[f"vbf{par}"]
                        tt.wait_ge(sm.h, vl)
                        for half, bank in ((0, ps3), (1, ps4)):
                            for ci in range(NCH):
                                tt.matmul(bank[:, 0:512],
                                          lnt_bf[:, ci * TF + s4 * 128:ci * TF + (s4 + 1) * 128],
                                          wv_sb[:, ci * C + half * 512:ci * C + (half + 1) * 512],
                                          start=(ci == 0), stop=False)
                            _li = tt.matmul(bank[:, 0:512], ones_row_bf[:],
                                      bv_row[:, half * 512:(half + 1) * 512],
                                      start=False, stop=True)
                        sPE.inc(_li)

                    v_pe = sPE.n

                    @block.vector
                    def _(v, par=par, v_pe=v_pe):
                        sPE.wait(v, v_pe)
                        sm, vl = Tp[f"vbf{par}"]
                        v.wait_ge(sm.h, vl)
                        v.tensor_copy(v_bf[:, par * C:par * C + 512], ps3[:, 0:512])
                        _li = v.tensor_copy(v_bf[:, par * C + 512:par * C + 1024], ps4[:, 0:512])
                        sDVE.inc(_li)

                    v_evac = sDVE.n
                    T["ps3"] = (sDVE, v_evac)
                    T["ps4"] = (sDVE, v_evac)

                    @block.tensor
                    def _(tt, s4=s4, lnt_bf=lnt_bf):
                        twait(tt, "ps1", "ps2")
                        for half, bank in ((0, ps1), (1, ps2)):
                            for ci in range(NCH):
                                tt.matmul(bank[:, 0:512],
                                          lnt_bf[:, ci * TF + s4 * 128:ci * TF + (s4 + 1) * 128],
                                          wq_sb[:, ci * C + half * 512:ci * C + (half + 1) * 512],
                                          start=(ci == 0), stop=False)
                            _li = tt.matmul(bank[:, 0:512], ones_row_bf[:],
                                      bq_row[:, half * 512:(half + 1) * 512],
                                      start=False, stop=True)
                        sPE.inc(_li)

                    q_pe = sPE.n

                    @block.scalar
                    def _(s, q_pe=q_pe):
                        sPE.wait(s, q_pe)
                        sm, vl = Tp["eq"]
                        s.wait_ge(sm.h, vl)
                        s.activation(eq_s[:, 0:512], ps1[:, 0:512], AF.Exp, bias=0.0, scale=1.0)
                        _li = s.activation(eq_s[:, 512:1024], ps2[:, 0:512], AF.Exp, bias=0.0, scale=1.0)
                        sACT.inc(_li)

                    q_exp = sACT.n
                    T["ps1"] = (sACT, q_exp)
                    T["ps2"] = (sACT, q_exp)

                    @block.vector
                    def _(v, par=par, q_exp=q_exp):
                        sACT.wait(v, q_exp)
                        sm, vl = Tp[f"qstm{par}"]
                        v.wait_ge(sm.h, vl)
                        eq3 = eq_s[:].rearrange("p (h d) -> p h d", d=D)
                        vedge(v, v.tensor_reduce(qrow[:, 0:H].rearrange("p (h o) -> p h o", o=1),
                                                 eq3, axis=AX.X, op=ALU.add))
                        vedge(v, v.reciprocal(qrow[:, H:2 * H], qrow[:, 0:H]))
                        qr3 = qrow[:, H:2 * H].rearrange("p (h o) -> p h o", o=1)
                        _li = v.tensor_tensor(
                            qs_tm[:, par * C:(par + 1) * C].rearrange("p (h d) -> p h d", d=D),
                            eq3, qr3.to_broadcast([128, H, D]), op=ALU.mult)
                        sDVE.inc(_li)

                    qs_done = sDVE.n
                    Tp["eq"] = (sDVE, qs_done)

                    if pend is not None:
                        emit_qs_tr_state(*pend)
                    pend = (s4, qs_done, v_evac)

                    if s4 == 1 and t + 1 < NT:
                        ln1_done[t + 1] = ln1_back(t + 1)

                emit_qs_tr_state(*pend)
                Tp[f"ln1_{par_t}"] = (sPE, sPE.n)

            pass1_pe = sPE.n
            pass1_dve = sDVE.n

            @block.vector
            def _(v):
                sPE.wait(v, pass1_pe)
                sDVE.wait(v, pass1_dve)
                vedge(v, v.reciprocal(z_sb[:, 8:16], z_sb[:, 0:8]))
                for h in range(H):
                    hb = (h % 2) * 64
                    hc = (h // 2) * D
                    _li = v.tensor_scalar(s_bf[hb:hb + 64, hc:hc + D], psE[hb:hb + 64, hc:hc + D],
                                    z_sb[hb:hb + 64, 8 + h // 2:8 + h // 2 + 1],
                                    None, op0=ALU.mult)
                sDVE.inc(_li)

            epi_done = sDVE.n
            T["psE"] = (sDVE, epi_done)

        with ExitStack() as p2ctx:
            sb2 = lambda n_, s_, d_: p2ctx.enter_context(nc.sbuf_tensor(n_, s_, d_))
            wo_sb = sb2("wo_sb", [128, NCH * C], BF16)
            attno2 = [sb2(f"attno{i}", [128, NCH * TF], F32) for i in range(2)]
            cbfA = [sb2(f"cbfA{i}", [128, NCH * TF], BF16) for i in range(2)]
            lnb = sb2("lnb", [128, NCH * TF], BF16)
            x2f = sb2("x2f", [128, NCH * TF], F32)
            xa3 = sb2("xa3", [128, 2 * TF], F32)
            x2tm = sb2("x2tm", [128, 2 * C], BF16)
            lntm = sb2("lntm", [128, 2 * C], BF16)
            lgrow = sb2("lgrow", [8, TF], F32)
            ones_row5_bf = sb2("ones_row5_bf", [1, TF], BF16)
            bcol_bf = sb2("bcol_bf", [128, 8], BF16)
            wob_row = sb2("wob_row", [1, C], BF16)

            @block.gpsimd
            def _(g):
                g.wait_ge(sPE.h, pass1_pe)
                srcwo = woT.rearrange("(c p) co -> p c co", p=128)
                dstwo = wo_sb[:].rearrange("p (c co) -> p c co", c=NCH)
                for i in range(2):
                    sWgt.incd(g.dma_start(out=dstwo[:, i * 4:(i + 1) * 4, :],
                                          in_=srcwo[:, i * 4:(i + 1) * 4, :]))

            wo_ready = (sWgt, sWgt.n)

            @block.vector
            def _(v):
                v.wait_ge(sDVE.h, epi_done)
                v.memset(ones_row5_bf[:], 1.0)
                _li = v.tensor_copy(bcol_bf[:], pp[:, PP_LNB:PP_LNB + 8])
                sDVE.inc(_li)

            p2pre_dve = sDVE.n

            wob_done = 0
            for rnd in range(2):
                @block.tensor
                def _(tt, rnd=rnd):
                    if rnd == 0:
                        tt.wait_ge(wo_ready[0].h, wo_ready[1])
                        sDVE.wait(tt, p2pre_dve)
                    twait(tt, "psE")
                    for co4 in range(4):
                        co = rnd * 4 + co4
                        for ci in range(NCH):
                            _li = tt.matmul(psE[0:1, co4 * 128:(co4 + 1) * 128],
                                            bcol_bf[:, ci:ci + 1],
                                            wo_sb[:, ci * C + co * 128:ci * C + (co + 1) * 128],
                                            start=(ci == 0), stop=(ci == NCH - 1))
                    sPE.inc(_li)

                p_wob = sPE.n

                @block.vector
                def _(v, rnd=rnd, p_wob=p_wob):
                    sPE.wait(v, p_wob)
                    _li = v.tensor_copy(wob_row[:, rnd * 512:(rnd + 1) * 512],
                                        psE[0:1, 0:512])
                    sDVE.inc(_li)

                wob_done = sDVE.n
                T["psE"] = (sDVE, wob_done)

            @block.vector
            def _(v):
                sDVE.wait(v, wob_done)
                for ci in range(NCH):
                    _li = v.tensor_scalar(wo_sb[:, ci * C:(ci + 1) * C],
                                          wo_sb[:, ci * C:(ci + 1) * C],
                                          pp[:, PP_LNG + ci:PP_LNG + ci + 1],
                                          None, op0=ALU.mult)
                sDVE.inc(_li)

            wo_scaled = sDVE.n

            Tq = {
                "attno0": (sPE, 0), "attno1": (sPE, 0),
                "cbfA0": [(sPE, 0)], "cbfA1": [(sPE, 0)],
                "lnb": (sPE, 0), "x2f": (sACT, 0), "x2f_dve": (sDVE, 0),
                "lgrow": (sPE, 0),
            }
            x2st = {}
            lnst = {}
            xchunk_load = {}
            ln2_norm = {}
            gate_pe_t = {}

            def tqwait(eng, *keys):
                for k in keys:
                    s, v = Tq[k]
                    eng.wait_ge(s.h, v)

            lna_state = {}
            ln2_state = {}

            def emit_A(t):
                par = t % 2
                att = cbfA[par]
                o_last = 0
                for co in range(NCH):
                    bank = ps1 if co % 2 == 0 else ps2
                    bkey = "ps1" if co % 2 == 0 else "ps2"

                    @block.tensor
                    def _(tt, co=co, bank=bank, bkey=bkey, t=t):
                        if t == 0 and co == 0:
                            sDVE.wait(tt, epi_done)
                        twait(tt, bkey)
                        for s4 in range(4):
                            for hh in range(2):
                                h = 2 * co + hh
                                hb = hh * 64
                                _li = tt.matmul(
                                    bank[hb:hb + 64, s4 * 128:(s4 + 1) * 128],
                                    s_bf[hb:hb + 64, (h // 2) * D:(h // 2 + 1) * D],
                                    qs_fm[hb:hb + 64,
                                          co * N + t * TF + s4 * 128:co * N + t * TF + (s4 + 1) * 128],
                                    start=True, stop=True)
                        sPE.inc(_li)

                    o_pe = sPE.n

                    @block.scalar
                    def _(s, co=co, bank=bank, o_pe=o_pe, par=par, att=att):
                        sPE.wait(s, o_pe)
                        if co == 0:
                            for sm, vl in Tq[f"cbfA{par}"]:
                                s.wait_ge(sm.h, vl)
                        _li = s.activation(att[:, co * TF:(co + 1) * TF], bank[:, 0:TF],
                                           AF.Copy, bias=0.0, scale=1.0)
                        sACT.inc(_li)

                    o_last = sACT.n
                    T[bkey] = (sACT, sACT.n)
                return o_last

            def emit_ln_front(t, att_f, do_cast, state, str_, bkA, bkB, kA, kB,
                              pre_sc, sum_src=None, epsap=None):
                epsap = epsap if epsap is not None else eps1
                """squares(+cast) ACT; sums PE; stats DVE; Ln/Exp ACT."""
                @block.scalar
                def _(s, t=t):
                    for sm, vl in pre_sc:
                        s.wait_ge(sm.h, vl)
                    twait(s, "sqbf")
                    if do_cast:
                        twait(s, "cbf")
                        for c in range(NCH):
                            s.activation(cbf[:, c * TF:(c + 1) * TF], att_f(c),
                                         AF.Copy, bias=0.0, scale=1.0)
                    for c in range(NCH):
                        _li = s.activation(sqbf[:, c * TF:(c + 1) * TF], att_f(c),
                                           AF.Square, bias=0.0, scale=1.0)
                    sACT.inc(_li)

                a_sq = sACT.n
                state["a_sq"] = a_sq
                ssrc = (lambda c: cbf[:, c * TF:(c + 1) * TF]) if do_cast else sum_src

                @block.tensor
                def _(tt):
                    sACT.wait(tt, a_sq)
                    twait(tt, kA, kB)
                    for c in range(NCH):
                        tt.matmul(bkA[0:1, 0:TF], ones_col_bf[:], ssrc(c),
                                  start=(c == 0), stop=(c == NCH - 1))
                    for c in range(NCH):
                        _li = tt.matmul(bkB[0:1, 0:TF], ones_col_bf[:],
                                        sqbf[:, c * TF:(c + 1) * TF],
                                        start=(c == 0), stop=(c == NCH - 1))
                    sPE.inc(_li)

                p_sums = sPE.n
                T["sqbf"] = (sPE, p_sums)
                if do_cast:
                    T["cbf"] = (sPE, p_sums)
                state["p_sums"] = p_sums

                @block.vector
                def _(v):
                    sPE.wait(v, p_sums)
                    mu, va = str_[:, 0:TF], str_[:, TF:2 * TF]
                    aa = str_[:, 2 * TF:3 * TF]
                    v.tensor_scalar(mu, bkA[0:1, 0:TF], 1.0 / C, None, op0=ALU.mult)
                    vedge(v, v.tensor_scalar(va, bkB[0:1, 0:TF], 1.0 / C, None, op0=ALU.mult))
                    vedge(v, v.tensor_tensor(aa, mu, mu, op=ALU.mult))
                    sDVE.inc(v.tensor_tensor(va, va, aa, op=ALU.subtract))

                d_var = sDVE.n

                @block.scalar
                def _(s):
                    sDVE.wait(s, d_var)
                    _li = s.activation(str_[:, 2 * TF:3 * TF], str_[:, TF:2 * TF],
                                       AF.Ln, bias=epsap[:], scale=1.0)
                    sACT.inc(_li)

                a_ln = sACT.n

                @block.scalar
                def _(s):
                    sACT.wait(s, a_ln)
                    _li = s.activation(str_[:, TF:2 * TF], str_[:, 2 * TF:3 * TF],
                                       AF.Exp, bias=0.0, scale=-0.5)
                    sACT.inc(_li)

                state["a_rstd"] = sACT.n

            def emit_ln_back(state, str_, bkA, bkB, emit_norm, onesap=None):
                onesap = onesap if onesap is not None else ones_row_f

                @block.tensor
                def _(tt):
                    sACT.wait(tt, state["a_rstd"])
                    tt.matmul(bkA[:, 0:TF], onesap[:], str_[:, 0:TF],
                              start=True, stop=True)
                    _li = tt.matmul(bkB[:, 0:TF], onesap[:], str_[:, TF:2 * TF],
                                    start=True, stop=True)
                    sPE.inc(_li)

                p_bc = sPE.n

                @block.vector
                def _(v):
                    sPE.wait(v, p_bc)
                    last = emit_norm(v, bkA[:, 0:TF], bkB[:, 0:TF])
                    sDVE.inc(last)

                return sDVE.n

            def emit_LNa_front(t, o_last):
                par = t % 2
                att = cbfA[par]
                lna_state[t] = {}
                emit_ln_front(t, lambda c: att[:, c * TF:(c + 1) * TF], False,
                              lna_state[t], strow, psLnA, psLnB, "psLnA", "psLnB",
                              pre_sc=[(sACT, o_last)],
                              sum_src=lambda c: att[:, c * TF:(c + 1) * TF])

            def emit_LNa_back(t):
                par = t % 2
                att = cbfA[par]

                def norm(v, mu, rstd):
                    tqwait(v, "lnb")
                    for c in range(NCH):
                        ins = v.tensor_tensor(lnb[:, c * TF:(c + 1) * TF],
                                              att[:, c * TF:(c + 1) * TF], mu,
                                              op=ALU.subtract)
                    vedge(v, ins)
                    for c in range(NCH):
                        ins = v.tensor_tensor(lnb[:, c * TF:(c + 1) * TF],
                                              lnb[:, c * TF:(c + 1) * TF], rstd,
                                              op=ALU.mult)
                    return ins

                done = emit_ln_back(lna_state[t], strow, psLnA, psLnB, norm)
                T["psLnA"] = (sDVE, done)
                T["psLnB"] = (sDVE, done)
                lna_state[t]["done"] = done
                Tq[f"cbfA{par}"] = [(sPE, lna_state[t]["p_sums"]), (sDVE, done)]

            def emit_B(t):
                x2_dve = {}
                for co in range(NCH):
                    bank = ps3 if co % 2 == 0 else ps4
                    bkey = "ps3" if co % 2 == 0 else "ps4"

                    @block.sync
                    def _(sy, co=co, t=t):
                        prev = (t, co - 2) if co >= 2 else (t - 1, co + NCH - 2)
                        if prev in xchunk_evac:
                            sy.wait_ge(sDVE.h, xchunk_evac[prev])
                        elif t == 0:
                            sy.wait_ge(sDVE.h, epi_done)
                        src = xT[co * 128:(co + 1) * 128, t * TF:(t + 1) * TF]
                        sXa[co % 2].incd(sy.dma_start(
                            out=xa3[:, (co % 2) * TF:(co % 2 + 1) * TF], in_=src))
                        xchunk_load[(t, co)] = (sXa[co % 2], sXa[co % 2].n)

                    @block.tensor
                    def _(tt, co=co, bank=bank, bkey=bkey, t=t):
                        if co == 0:
                            sDVE.wait(tt, wo_scaled)
                            sDVE.wait(tt, lna_state[t]["done"])
                        twait(tt, bkey)
                        for ci in range(NCH):
                            tt.matmul(bank[:, 0:TF],
                                      wo_sb[:, ci * C + co * 128:ci * C + (co + 1) * 128],
                                      lnb[:, ci * TF:(ci + 1) * TF],
                                      start=(ci == 0), stop=False)
                        _li = tt.matmul(bank[:, 0:TF],
                                        wob_row[:, co * 128:(co + 1) * 128],
                                        ones_row5_bf[:], start=False, stop=True)
                        sPE.inc(_li)

                    wo_pe = sPE.n
                    if co == NCH - 1:
                        Tq["lnb"] = (sPE, wo_pe)

                    @block.vector
                    def _(v, co=co, bank=bank, t=t, wo_pe=wo_pe):
                        sPE.wait(v, wo_pe)
                        xs, xv = xchunk_load[(t, co)]
                        v.wait_ge(xs.h, xv)
                        if co == 0:
                            tqwait(v, "x2f", "x2f_dve")
                        vedge(v, v.tensor_scalar(x2f[:, co * TF:(co + 1) * TF],
                                                 bank[:, 0:TF],
                                                 pp[:, PP_BO + co:PP_BO + co + 1],
                                                 None, op0=ALU.add))
                        _li = v.tensor_tensor(x2f[:, co * TF:(co + 1) * TF],
                                              x2f[:, co * TF:(co + 1) * TF],
                                              xa3[:, (co % 2) * TF:(co % 2 + 1) * TF],
                                              op=ALU.add)
                        sDVE.inc(_li)

                    x2_dve[(t, co)] = sDVE.n
                    xchunk_evac[(t, co)] = sDVE.n
                    T[bkey] = (sDVE, sDVE.n)
                return sDVE.n

            def emit_x2tr(t, x2_done):
                for s4 in range(4):
                    @block.tensor
                    def _(tt, s4=s4, x2_done=x2_done):
                        if s4 == 0:
                            sDVE.wait(tt, x2_done)
                        twait(tt, "ps1", "ps2")
                        for c in range(NCH):
                            bank = ps1 if c < 4 else ps2
                            _li = tt.transpose(bank[:, (c % 4) * 128:(c % 4 + 1) * 128],
                                         x2f[:, c * TF + s4 * 128:c * TF + (s4 + 1) * 128],
                                         consts[:, 0:128])
                        sPE.inc(_li)

                    tr_pe = sPE.n

                    @block.vector
                    def _(v, s4=s4, t=t, tr_pe=tr_pe):
                        sPE.wait(v, tr_pe)
                        b = t * 4 + s4
                        if b - 1 in x2st:
                            v.wait_ge(sStX.h, x2st[b - 1])
                        for c in range(NCH):
                            bank = ps1 if c < 4 else ps2
                            _li = v.tensor_copy(x2tm[:, (s4 % 2) * C + c * 128:(s4 % 2) * C + (c + 1) * 128],
                                          bank[:, (c % 4) * 128:(c % 4 + 1) * 128])
                        sDVE.inc(_li)

                    x2tr_d = sDVE.n
                    T["ps1"] = (sDVE, sDVE.n)
                    T["ps2"] = (sDVE, sDVE.n)

                    @block.sync
                    def _(sy, s4=s4, t=t, x2tr_d=x2tr_d):
                        sy.wait_ge(sDVE.h, x2tr_d)
                        row0 = t * TF + s4 * 128
                        sStX.incd(sy.dma_start(out=x2_dram[row0:row0 + 128, :],
                                               in_=x2tm[:, (s4 % 2) * C:(s4 % 2 + 1) * C]))

                    x2st[t * 4 + s4] = sStX.n
                return sPE.n

            def emit_LN2_front(t, x2_done):
                ln2_state[t] = {}
                emit_ln_front(t, lambda c: x2f[:, c * TF:(c + 1) * TF], True,
                              ln2_state[t], strow2, ps3, ps4, "ps3", "ps4",
                              pre_sc=[(sDVE, x2_done)], epsap=eps1_64)
                Tq["x2f"] = (sACT, ln2_state[t]["a_sq"])

            def emit_LN2_back(t):
                par = t % 2
                att = attno2[par]

                def norm(v, mu, rstd):
                    tqwait(v, f"attno{par}")
                    for c in range(NCH):
                        ins = v.tensor_tensor(att[:, c * TF:(c + 1) * TF],
                                              x2f[:, c * TF:(c + 1) * TF], mu,
                                              op=ALU.subtract)
                    vedge(v, ins)
                    for c in range(NCH):
                        ins = v.tensor_tensor(att[:, c * TF:(c + 1) * TF],
                                              att[:, c * TF:(c + 1) * TF], rstd,
                                              op=ALU.mult)
                    return ins

                done = emit_ln_back(ln2_state[t], strow2, ps3, ps4, norm,
                                    onesap=ones_row_f64)
                T["ps3"] = (sDVE, done)
                T["ps4"] = (sDVE, done)
                ln2_norm[t] = done
                Tq["x2f_dve"] = (sDVE, done)

            def emit_gate(t):
                par = t % 2
                att = attno2[par]

                @block.tensor
                def _(tt, t=t):
                    sDVE.wait(tt, ln2_norm[t])
                    twait(tt, "psE")
                    for ci in range(NCH):
                        _li = tt.matmul(psE[0:8, 0:TF],
                                  pp[:, PP_WG + ci * 8:PP_WG + (ci + 1) * 8],
                                  att[:, ci * TF:(ci + 1) * TF],
                                  start=(ci == 0), stop=(ci == NCH - 1))
                    sPE.inc(_li)

                gate_pe = sPE.n
                gate_pe_t[t] = gate_pe
                Tq[f"attno{par}"] = (sPE, gate_pe)

                @block.vector
                def _(v, gate_pe=gate_pe):
                    sPE.wait(v, gate_pe)
                    tqwait(v, "lgrow")
                    _li = v.tensor_copy(lgrow[:], psE[0:8, 0:TF])
                    sDVE.inc(_li)

                lg_dve = sDVE.n

                @block.tensor
                def _(tt, lg_dve=lg_dve):
                    sDVE.wait(tt, lg_dve)
                    for s4 in range(4):
                        _li = tt.transpose(psE[:, 256 + s4 * 8:256 + (s4 + 1) * 8],
                                           lgrow[0:8, s4 * 128:(s4 + 1) * 128],
                                           consts[0:8, 0:8])
                    sPE.inc(_li)

                lgt_pe = sPE.n
                Tq["lgrow"] = (sPE, lgt_pe)

                @block.vector
                def _(v, t=t, lgt_pe=lgt_pe):
                    sPE.wait(v, lgt_pe)
                    _li = v.tensor_copy(glog[:, t * 4 * E:(t + 1) * 4 * E],
                                        psE[:, 256:288])
                    sDVE.inc(_li)

                T["psE"] = (sDVE, sDVE.n)

            def emit_ln2bf_tr(t):
                par = t % 2
                att = attno2[par]

                @block.scalar
                def _(s, t=t):
                    sDVE.wait(s, ln2_norm[t])
                    twait(s, "cbf")
                    for c in range(NCH):
                        _li = s.activation(cbf[:, c * TF:(c + 1) * TF],
                                     att[:, c * TF:(c + 1) * TF],
                                     AF.Copy, bias=0.0, scale=1.0)
                    sACT.inc(_li)

                ln2bf_act = sACT.n

                for s4 in range(4):
                    @block.tensor
                    def _(tt, s4=s4, ln2bf_act=ln2bf_act):
                        if s4 == 0:
                            sACT.wait(tt, ln2bf_act)
                        twait(tt, "pstr")
                        for c in range(NCH):
                            _li = tt.transpose(pstr[:, c * 128:(c + 1) * 128],
                                         cbf[:, c * TF + s4 * 128:c * TF + (s4 + 1) * 128],
                                         idn_bf[:])
                        sPE.inc(_li)

                    ltr_pe = sPE.n
                    if s4 == 3:
                        T["cbf"] = (sPE, ltr_pe)

                    @block.vector
                    def _(v, s4=s4, t=t, ltr_pe=ltr_pe):
                        sPE.wait(v, ltr_pe)
                        b = t * 4 + s4
                        if b - 1 in lnst:
                            v.wait_ge(sStL.h, lnst[b - 1])
                        for c in range(NCH):
                            _li = v.tensor_copy(lntm[:, (s4 % 2) * C + c * 128:(s4 % 2) * C + (c + 1) * 128],
                                          pstr[:, c * 128:(c + 1) * 128])
                        sDVE.inc(_li)

                    lntr_d = sDVE.n
                    T["pstr"] = (sDVE, sDVE.n)

                    @block.sync
                    def _(sy, s4=s4, t=t, lntr_d=lntr_d):
                        sy.wait_ge(sDVE.h, lntr_d)
                        row0 = t * TF + s4 * 128
                        sStL.incd(sy.dma_start(out=ln2_cin[row0:row0 + 128, :],
                                               in_=lntm[:, (s4 % 2) * C:(s4 % 2 + 1) * C]))

                    lnst[t * 4 + s4] = sStL.n

                tpc = CHK_T // TF
                if (t + 1) % tpc == 0:
                    ck = (t + 1) // tpc - 1

                    @block.gpsimd
                    def _(g, ck=ck, lv=lnst[t * 4 + 3]):
                        g.wait_ge(sStL.h, lv)
                        ccTok[ck].inc(g.collective_compute(
                            "AllGather", ALU.bypass, replica_groups=RG,
                            ins=[ln2_cin[ck * CHK_T:(ck + 1) * CHK_T, :].opt()],
                            outs=[tok_all[ck * NCORES * CHK_T:(ck + 1) * NCORES * CHK_T, :].opt()]))

            xchunk_evac = {}
            x2_done_t = {}
            for t in range(NT):
                o_last = emit_A(t)
                emit_LNa_front(t, o_last)
                if t >= 1:
                    x2_done_t[t - 1] = emit_B(t - 1)
                    emit_x2tr(t - 1, x2_done_t[t - 1])
                    emit_LN2_front(t - 1, x2_done_t[t - 1])
                emit_LNa_back(t)
                if t >= 1:
                    emit_LN2_back(t - 1)
                    emit_gate(t - 1)
                    emit_ln2bf_tr(t - 1)

            tl = NT - 1
            x2_done_t[tl] = emit_B(tl)
            emit_x2tr(tl, x2_done_t[tl])
            emit_LN2_front(tl, x2_done_t[tl])
            emit_LN2_back(tl)
            emit_gate(tl)
            emit_ln2bf_tr(tl)

            pass2_x2st = sStX.n
            pass2_lnst = sStL.n
            pass2_dve = sDVE.n

            @block.vector
            def _(v):
                sDVE.wait(v, pass2_dve)
                g3 = glog[:].rearrange("p (s e) -> p s e", e=E)
                m = pos_s[:]
                vedge(v, v.tensor_reduce(m.rearrange("p (s o) -> p s o", o=1), g3,
                                         axis=AX.X, op=ALU.max))
                mb = m.rearrange("p (s o) -> p s o", o=1).to_broadcast([128, NSUB, E])
                t3 = scr[:].rearrange("p (s e) -> p s e", e=E)
                _li = v.tensor_tensor(t3, g3, mb, op=ALU.subtract)
                sDVE.inc(_li)

            g_dve1 = sDVE.n

            @block.scalar
            def _(s):
                sDVE.wait(s, g_dve1)
                sACT.inc(s.activation(base_bc[:], scr[:], AF.Exp, bias=0.0, scale=1.0))

            g_act = sACT.n

            @block.vector
            def _(v):
                sACT.wait(v, g_act)
                ex3 = base_bc[:].rearrange("p (s e) -> p s e", e=E)
                sm = gp_s[:]
                vedge(v, v.tensor_reduce(sm.rearrange("p (s o) -> p s o", o=1), ex3,
                                         axis=AX.X, op=ALU.add))
                vedge(v, v.reciprocal(grec[:], sm))
                g3 = glog[:].rearrange("p (s e) -> p s e", e=E)
                mb = pos_s[:].rearrange("p (s o) -> p s o", o=1).to_broadcast([128, NSUB, E])
                msk3 = scr[:].rearrange("p (s e) -> p s e", e=E)
                vedge(v, v.tensor_tensor(msk3, g3, mb, op=ALU.is_ge))
                vedge(v, v.tensor_scalar(oh_s[:], scr[:], -64.0, 64.0, op0=ALU.mult, op1=ALU.add))
                iob = iota8_row[:].rearrange("p (o e) -> p o e", o=1).to_broadcast([128, NSUB, E])
                cand3 = oh_s[:].rearrange("p (s e) -> p s e", e=E)
                vedge(v, v.tensor_tensor(cand3, cand3, iob, op=ALU.add))
                vedge(v, v.tensor_reduce(eidx[:].rearrange("p (s o) -> p s o", o=1), cand3,
                                         axis=AX.X, op=ALU.min))
                eb = eidx[:].rearrange("p (s o) -> p s o", o=1).to_broadcast([128, NSUB, E])
                oh3 = oh_s[:].rearrange("p (s e) -> p s e", e=E)
                _li = v.tensor_tensor(oh3, iob, eb, op=ALU.is_equal)
                sDVE.inc(_li)

            g_done = sDVE.n

            @block.tensor
            def _(tt):
                sDVE.wait(tt, g_done)
                twait(tt, "ps1", "ps2")
                for s in range(NSUB):
                    _li = tt.matmul(ps1[:, s * 8:(s + 1) * 8],
                              consts[:, 128:256], oh_s[:, s * E:(s + 1) * E],
                              start=True, stop=True)
                sPE.inc(_li)

            lp_pe = sPE.n

            @block.vector
            def _(v):
                sPE.wait(v, lp_pe)
                _li = v.tensor_copy(lpos_blk[:, 0:NSUB * E], ps1[:, 0:NSUB * E])
                sDVE.inc(_li)

            lp_dve = sDVE.n
            T["ps1"] = (sDVE, lp_dve)

            @block.tensor
            def _(tt):
                sDVE.wait(tt, g_done)
                twait(tt, "ps2")
                for s in range(NSUB):
                    _li = tt.matmul(ps2[0:8, s:s + 1], oh_s[:, s * E:(s + 1) * E],
                              ones_col_f[:], start=True, stop=True)
                sPE.inc(_li)

            cnt_pe = sPE.n

            @block.vector
            def _(v):
                sPE.wait(v, cnt_pe)
                v.wait_ge(sGP.h, p0_gp)
                for s in range(NSUB):
                    _li = v.tensor_tensor(carr[:, s + 1:s + 2], carr[:, s:s + 1],
                                          ps2[0:8, s:s + 1], op=ALU.add)
                    if s != NSUB - 1:
                        vedge(v, _li)
                sDVE.inc(_li)

            cum_done = sDVE.n
            T["ps2"] = (sDVE, cum_done)

            @block.sync
            def _(sy):
                sy.wait_ge(sDVE.h, cum_done)
                sStC.incd(sy.dma_start(out=cnt_cin.rearrange("o e -> e o"),
                                       in_=carr[0:8, NSUB:NSUB + 1]))

            cnt_store = sStC.n

            @block.gpsimd
            def _(g):
                g.wait_ge(sStC.h, cnt_store)
                ccCnt.inc(g.collective_compute(
                    "AllGather", ALU.bypass, replica_groups=RG,
                    ins=[cnt_cin.opt()], outs=[cnt_all.opt()]))
                g.wait_ge(ccCnt.h, 1)
                sCnL.incd(g.dma_start(out=cnt_sb[:, :], in_=cnt_all[:, :]))

            cnt_load = sCnL.n

            @block.tensor
            def _(tt):
                tt.wait_ge(sCnL.h, cnt_load)
                twait(tt, "ps4")
                _li = tt.matmul(ps4[0:1, 0:8], pp[0:8, PP_PREF:PP_PREF + 1],
                          cnt_sb[:, :], start=True, stop=True)
                sPE.inc(_li)

            off_pe = sPE.n

            @block.vector
            def _(v):
                sPE.wait(v, off_pe)
                _li = v.tensor_copy(orow8[:], ps4[0:1, 0:8])
                sDVE.inc(_li)

            off_dve = sDVE.n

            @block.tensor
            def _(tt):
                sDVE.wait(tt, off_dve)
                _li = tt.matmul(ps4[0:8, 8:9], orow8[:], one_f[:], start=True, stop=True)
                sPE.inc(_li)

            oc_pe = sPE.n

            @block.vector
            def _(v):
                sPE.wait(v, oc_pe)
                _li = v.tensor_scalar(basec[:], carr[0:8, 0:NSUB], ps4[0:8, 8:9],
                                None, op0=ALU.add)
                sDVE.inc(_li)

            base_dve = sDVE.n
            T["ps4"] = (sDVE, base_dve)

            @block.sync
            def _(sy):
                sy.wait_ge(sDVE.h, base_dve)
                sStC.incd(sy.dma_start(
                    out=basec_d.rearrange("o (s e) -> e (s o)", e=E),
                    in_=basec[:], allow_slow_non_contiguous=True))

            basec_st = sStC.n

            @block.sync
            def _(sy):
                sy.wait_ge(sStC.h, basec_st)
                sBr.incd(sy.dma_start(out=base_row[:], in_=basec_d[:]))

            baser_ld = sBr.n

            @block.tensor
            def _(tt):
                tt.wait_ge(sBr.h, baser_ld)
                twait(tt, "ps2")
                _li = tt.matmul(ps2[:, 0:NSUB * E], ones_row_f[:], base_row[:],
                          start=True, stop=True)
                sPE.inc(_li)

            bb_pe = sPE.n

            @block.vector
            def _(v):
                sPE.wait(v, bb_pe)
                _li = v.tensor_copy(base_bc[:, 0:NSUB * E], ps2[:, 0:NSUB * E])
                sDVE.inc(_li)

            bb_dve = sDVE.n
            T["ps2"] = (sDVE, bb_dve)

            @block.vector
            def _(v):
                sDVE.wait(v, bb_dve)
                sDVE.wait(v, lp_dve)
                vedge(v, v.tensor_tensor(scr[:], lpos_blk[:], base_bc[:], op=ALU.add))
                vedge(v, v.tensor_tensor(scr[:], scr[:], oh_s[:], op=ALU.mult))
                s3v = scr[:].rearrange("p (s e) -> p s e", e=E)
                vedge(v, v.tensor_reduce(pos_s[:].rearrange("p (s o) -> p s o", o=1), s3v,
                                         axis=AX.X, op=ALU.add))
                keep = lpos_blk[:, 0:NSUB]
                vedge(v, v.tensor_scalar(keep, pos_s[:], float(CAP), None, op0=ALU.is_lt))
                vedge(v, v.tensor_tensor(gp_s[:], grec[:], keep, op=ALU.mult))
                dstf = lpos_blk[:, NSUB:2 * NSUB]
                vedge(v, v.tensor_scalar(dstf, eidx[:], float(CAP), None, op0=ALU.mult))
                vedge(v, v.tensor_tensor(dstf, dstf, pos_s[:], op=ALU.add))
                pen = lpos_blk[:, 2 * NSUB:3 * NSUB]
                vedge(v, v.tensor_scalar(pen, keep, -BIG, BIG, op0=ALU.mult, op1=ALU.add))
                vedge(v, v.tensor_tensor(dstf, dstf, pen, op=ALU.add))
                vedge(v, v.tensor_copy(dst_u[:], dstf))
                cf = lpos_blk[:, 3 * NSUB:4 * NSUB]
                ctmp = lpos_blk[:, 5 * NSUB:6 * NSUB]
                vedge(v, v.tensor_scalar(cf, pos_s[:], float(CHK_E), None, op0=ALU.is_ge))
                for kk in range(2, NCHK_E):
                    vedge(v, v.tensor_scalar(ctmp, pos_s[:], float(kk * CHK_E), None, op0=ALU.is_ge))
                    vedge(v, v.tensor_tensor(cf, cf, ctmp, op=ALU.add))
                d2f = lpos_blk[:, 4 * NSUB:5 * NSUB]
                vedge(v, v.tensor_scalar(d2f, eidx[:], float(CHK_E), None, op0=ALU.mult))
                vedge(v, v.tensor_tensor(d2f, d2f, pos_s[:], op=ALU.add))
                vedge(v, v.tensor_scalar(cf, cf, float((E - 1) * CHK_E), None, op0=ALU.mult))
                vedge(v, v.tensor_tensor(d2f, d2f, cf, op=ALU.add))
                vedge(v, v.tensor_tensor(d2f, d2f, pen, op=ALU.add))
                _li = v.tensor_copy(dst2_u[:], d2f)
                sDVE.inc(_li)

            route_done = sDVE.n

            @block.gpsimd
            def _(g):
                g.wait_ge(sDVE.h, route_done)
                g.wait_ge(sStZ.h, src_zero_st)
                g.wait_ge(sPP.h, p0_ls)
                for s in range(NSUB):
                    sSc.incd(g.indirect_dma_start(
                        out=src_cin[:, :], out_offset=IOoA(ap=dst_u[:, s:s + 1], axis=0),
                        in_=gidx_all[:, s:s + 1], in_offset=None,
                        bounds_check=bcreg[0], oob_is_err=False))
                g.wait_ge(sSc.h, sSc.n)
                ccSrc.inc(g.collective_compute(
                    "AllReduce", ALU.add, replica_groups=RG,
                    ins=[src_cin.opt()], outs=[src_all.opt()]))

        p12.close()

        with ExitStack() as fctx:
            sbf = lambda n_, s_, d_: fctx.enter_context(nc.sbuf_tensor(n_, s_, d_))
            stgF = sbf("stgF", [128, 2 * 2048], F32)
            w2_sb = sbf("w2_sb", [128, NFC * C], BF16)
            pstr2 = fctx.enter_context(nc.psum_tensor("pstr2", [128, 1024], BF16))
            w1_sb = sbf("w1_sb", [128, 2 * NCH * 1024], BF16)
            srcb = sbf("srcb", [CAPB, 128], F32)
            src_f = sbf("src_f", [128, CAPB], F32)
            src_u = sbf("src_u", [128, CAPB], U32)
            cslot_s = sbf("cslot_s", [CAPB, 1], U32)
            g_tm = sbf("g_tm", [128, 4 * C], BF16)
            g_fm = sbf("g_fm", [128, 2 * NCH * ST], BF16)
            h_bf = sbf("h_bf", [128, NFC * ST], BF16)
            y_bf = sbf("y_bf", [128, NCH * ST], BF16)
            ytm = sbf("ytm", [128, 4 * C], BF16)

            castF = StagedCaster(stgF, half=2048)
            fenceF = ((sStX, pass2_x2st), (sStL, pass2_lnst))
            w1_load_n = {}

            def emit_w1_load(gi, pe_wait):
                fr = gi % 4
                srcw1 = w1e.rearrange("(c p) f -> p c f", p=128)[:, :, fr * 1024:(fr + 1) * 1024]
                cw = ((sPE, pe_wait),) if pe_wait else ()
                swl = fenceF if not pe_wait else ()
                for r in range(4):
                    n = castF.emit(
                        srcw1[:, r * 2:(r + 1) * 2, :],
                        w1_sb[:, (gi % 2) * NCH * 1024 + r * 2048:
                              (gi % 2) * NCH * 1024 + (r + 1) * 2048], 2,
                        sync_waits=swl, cast_waits=cw)
                w1_load_n[gi] = n

            emit_w1_load(0, 0)
            emit_w1_load(1, 0)

            srcw2 = w2e.rearrange("(q p) co -> p q co", p=128)
            for i in range(16):
                w2_ready = castF.emit(
                    srcw2[:, i * 2:(i + 1) * 2, :],
                    w2_sb[:, i * 2 * C:(i + 1) * 2 * C], 2,
                    sync_waits=(fenceF if i < 2 else ()))

            @block.sync
            def _(sy):
                sy.wait_ge(sStX.h, pass2_x2st)
                sy.wait_ge(sStL.h, pass2_lnst)
                sCsl.incd(sy.dma_start(out=cslot_s[:], in_=cslotu_d[:]))

            cslot_ld = sCsl.n

            @block.gpsimd
            def _(g):
                g.wait_ge(ccSrc.h, 1)
                g.wait_ge(sCsl.h, cslot_ld)
                g.wait_ge(sStX.h, pass2_x2st)
                g.wait_ge(sStL.h, pass2_lnst)
                sv = src_all.rearrange("(b p) o -> b (p o)", p=128)
                sSc.incd(g.indirect_dma_start(
                    out=srcb[0:CAPB, :], out_offset=None,
                    in_=sv, in_offset=IOoA(ap=cslot_s[0:CAPB, 0:1], axis=0),
                    bounds_check=bcreg[2], oob_is_err=False))

            src_ld = sSc.n

            @block.tensor
            def _(tt):
                tt.wait_ge(sSc.h, src_ld)
                twait(tt, "ps1")
                _li = tt.matmul(ps1[0:128, 0:CAPB], srcb[0:CAPB, :],
                          consts[0:CAPB, 0:CAPB], start=True, stop=True)
                sPE.inc(_li)

            srcT_pe = sPE.n

            @block.vector
            def _(v):
                sPE.wait(v, srcT_pe)
                vedge(v, v.tensor_copy(src_f[:], ps1[:, 0:CAPB]))
                _li = v.tensor_copy(src_u[:], src_f[:])
                sDVE.inc(_li)

            src_done = sDVE.n
            T["ps1"] = (sDVE, src_done)

            Tf = {
                "gtm0": (sPE, 0), "gtm1": (sPE, 0), "gtm2": (sPE, 0), "gtm3": (sPE, 0),
                "gfm0": (sPE, 0), "gfm1": (sPE, 0), "hbf": (sPE, 0),
                "ybf0": (sDVE, 0), "ybf1": (sDVE, 0),
                "ytm": (sStE, 0),
            }

            def tfwait(eng, *keys):
                for k in keys:
                    s, v = Tf[k]
                    eng.wait_ge(s.h, v)

            for st in range(NST):
                for s4 in range(4):
                    a = st * 4 + s4
                    gp4 = a % 4

                    @block.gpsimd
                    def _(g, a=a, gp4=gp4):
                        if a == 0:
                            g.wait_ge(sDVE.h, src_done)
                            for q in range(NCHK_T):
                                g.wait_ge(ccTok[q].h, 1)
                            g.wait_ge(sStX.h, pass2_x2st)
                        sm, vl = Tf[f"gtm{gp4}"]
                        g.wait_ge(sm.h, vl)
                        sGt[gp4].incd(g.indirect_dma_start(
                            out=g_tm[:, gp4 * C:(gp4 + 1) * C], out_offset=None,
                            in_=tok_all[:, :], in_offset=IOoA(ap=src_u[:, a:a + 1], axis=0),
                            bounds_check=bcreg[1], oob_is_err=False))

                    g_load = (sGt[gp4], sGt[gp4].n)

                    @block.tensor
                    def _(tt, s4=s4, gp4=gp4, g_load=g_load):
                        gs_, gv_ = g_load
                        tt.wait_ge(gs_.h, gv_)
                        twait(tt, "pstr2")
                        if s4 == 0:
                            tfwait(tt, f"gfm{st % 2}")
                        for c in range(NCH):
                            _li = tt.transpose(pstr2[:, c * 128:(c + 1) * 128],
                                         g_tm[:, gp4 * C + c * 128:gp4 * C + (c + 1) * 128],
                                         idn_bf[:])
                        sPE.inc(_li)

                    Tf[f"gtm{gp4}"] = (sPE, sPE.n)
                    gtr_pe = sPE.n

                    @block.vector
                    def _(v, s4=s4, st=st, gtr_pe=gtr_pe):
                        sPE.wait(v, gtr_pe)
                        gb = (st % 2) * NCH * ST
                        for c in range(NCH):
                            _li = v.tensor_copy(g_fm[:, gb + c * ST + s4 * 128:gb + c * ST + (s4 + 1) * 128],
                                          pstr2[:, c * 128:(c + 1) * 128])
                        sDVE.inc(_li)

                    T["pstr2"] = (sDVE, sDVE.n)
                gfm_ready = sDVE.n

                for fr in range(4):
                    gi = st * 4 + fr
                    buf = gi % 2
                    for fc in range(NCH):
                        fg = fr * NCH + fc
                        par = fg % 2
                        bank = ps1 if par == 0 else ps2
                        bkey = "ps1" if par == 0 else "ps2"

                        @block.tensor
                        def _(tt, fr=fr, fc=fc, fg=fg, bank=bank, bkey=bkey,
                              gi=gi, buf=buf, gfm_ready=gfm_ready):
                            if fc == 0:
                                ws, wv_ = w1_load_n[gi]
                                tt.wait_ge(ws.h, wv_)
                                if fr == 0:
                                    sDVE.wait(tt, gfm_ready)
                            twait(tt, bkey)
                            if fg == 0:
                                tfwait(tt, "hbf")
                            gb = (st % 2) * NCH * ST
                            for ci in range(NCH):
                                _li = tt.matmul(bank[:, 0:ST],
                                          w1_sb[:, buf * NCH * 1024 + ci * 1024 + fc * 128:
                                                buf * NCH * 1024 + ci * 1024 + (fc + 1) * 128],
                                          g_fm[:, gb + ci * ST:gb + (ci + 1) * ST],
                                          start=(ci == 0), stop=(ci == NCH - 1))
                            sPE.inc(_li)

                        h_pe = sPE.n

                        @block.scalar
                        def _(s, fg=fg, bank=bank, h_pe=h_pe):
                            sPE.wait(s, h_pe)
                            _li = s.activation(h_bf[:, fg * ST:(fg + 1) * ST],
                                         bank[:, 0:ST],
                                         AF.Relu, bias=pp[:, PP_B1 + fg:PP_B1 + fg + 1],
                                         scale=1.0)
                            sACT.inc(_li)

                        T[bkey] = (sACT, sACT.n)

                    if gi + 2 < NST * 4:
                        emit_w1_load(gi + 2, sPE.n)

                h_done_act = sACT.n
                Tf[f"gfm{st % 2}"] = (sPE, sPE.n)

                for co in range(NCH):
                    par = co % 2
                    bank = ps3 if par == 0 else ps4
                    bkey = "ps3" if par == 0 else "ps4"

                    @block.tensor
                    def _(tt, co=co, bank=bank, bkey=bkey, h_done_act=h_done_act):
                        if co == 0:
                            sACT.wait(tt, h_done_act)
                            tt.wait_ge(w2_ready[0].h, w2_ready[1])
                        twait(tt, bkey)
                        for fq in range(NFC):
                            _li = tt.matmul(bank[:, 0:ST],
                                      w2_sb[:, fq * C + co * 128:fq * C + (co + 1) * 128],
                                      h_bf[:, fq * ST:(fq + 1) * ST],
                                      start=(fq == 0), stop=(fq == NFC - 1))
                        sPE.inc(_li)

                    y_pe = sPE.n
                    if co == NCH - 1:
                        Tf["hbf"] = (sPE, y_pe)

                    @block.vector
                    def _(v, co=co, bank=bank, y_pe=y_pe):
                        sPE.wait(v, y_pe)
                        tfwait(v, f"ybf{co % 2}")
                        _li = v.tensor_scalar(y_bf[:, co * ST:(co + 1) * ST],
                                        bank[:, 0:ST],
                                        pp[:, PP_B2 + co:PP_B2 + co + 1], None, op0=ALU.add)
                        sDVE.inc(_li)

                    Tf[f"ybf{co % 2}"] = (sDVE, sDVE.n)
                    T[bkey] = (sDVE, sDVE.n)
                y_done = sDVE.n

                ytr_dve = {}
                for co in range(NCH):
                    @block.tensor
                    def _(tt, co=co, y_done=y_done):
                        if co == 0:
                            sDVE.wait(tt, y_done)
                            tfwait(tt, "ytm")
                        twait(tt, "pstr")
                        for s4 in range(4):
                            slot = (co % 2) * 4 + s4
                            _li = tt.transpose(pstr[:, slot * 128:(slot + 1) * 128],
                                         y_bf[:, co * ST + s4 * 128:co * ST + (s4 + 1) * 128],
                                         idn_bf[:])
                        sPE.inc(_li)

                    ytr_pe = sPE.n
                    if co == NCH - 1:
                        for par in range(2):
                            Tf[f"ybf{par}"] = (sPE, ytr_pe)

                    @block.vector
                    def _(v, co=co, ytr_pe=ytr_pe):
                        sPE.wait(v, ytr_pe)
                        for s4 in range(4):
                            slot = (co % 2) * 4 + s4
                            _li = v.tensor_copy(ytm[:, s4 * C + co * 128:s4 * C + (co + 1) * 128],
                                          pstr[:, slot * 128:(slot + 1) * 128])
                        sDVE.inc(_li)

                    ytr_dve[co] = sDVE.n
                    T["pstr"] = (sDVE, sDVE.n)

                @block.scalar
                def _(sc, st=st):
                    sc.wait_ge(sDVE.h, ytr_dve[NCH - 1])
                    for s4 in range(4):
                        row0 = st * ST + s4 * 128
                        sStE.incd(sc.dma_start(out=eout_cin[row0:row0 + 128, :],
                                               in_=ytm[:, s4 * C:(s4 + 1) * C]))

                Tf["ytm"] = (sStE, sStE.n)

                spc = CHK_E // ST
                if (st + 1) % spc == 0:
                    ck = (st + 1) // spc - 1

                    @block.gpsimd
                    def _(g, ck=ck, ev=sStE.n):
                        g.wait_ge(sStE.h, ev)
                        ccEout[ck].inc(g.collective_compute(
                            "AllGather", ALU.bypass, replica_groups=RG,
                            ins=[eout_cin[ck * CHK_E:(ck + 1) * CHK_E, :].opt()],
                            outs=[eout_all[ck * E * CHK_E:(ck + 1) * E * CHK_E, :].opt()]))

            ffn_st = sStE.n

        RING = 8
        with ExitStack() as hctx:
            erow = hctx.enter_context(nc.sbuf_tensor("erow", [128, RING * C], BF16))
            x2row = hctx.enter_context(nc.sbuf_tensor("x2row", [128, RING * C], BF16))
            orow = hctx.enter_context(nc.sbuf_tensor("orow", [128, RING * C], F32))

            @block.vector
            def _(v):
                v.wait_ge(sStE.h, ffn_st)
                sDVE.inc(v.memset(erow[:], 0.0))

            erow_init = sDVE.n
            comb_dve = {}
            e_load = {}
            x_load = {}
            o_store = {}
            for s in range(NSUB):
                par = s % RING

                @block.gpsimd
                def _(g, s=s, par=par):
                    if s == 0:
                        for q in range(NCHK_E):
                            g.wait_ge(ccEout[q].h, 1)
                        g.wait_ge(sDVE.h, erow_init)
                    if s >= RING:
                        g.wait_ge(sDVE.h, comb_dve[s - RING])
                    sEr[par].incd(g.indirect_dma_start(
                        out=erow[:, par * C:(par + 1) * C], out_offset=None,
                        in_=eout_all[:, :], in_offset=IOoA(ap=dst2_u[:, s:s + 1], axis=0),
                        bounds_check=bcreg[0], oob_is_err=False))

                e_load[s] = (sEr[par], sEr[par].n)

                @block.sync
                def _(sy, s=s, par=par):
                    if s >= RING:
                        sy.wait_ge(sDVE.h, comb_dve[s - RING])
                    else:
                        sy.wait_ge(sStE.h, ffn_st)
                        sy.wait_ge(sStX.h, pass2_x2st)
                    row0 = s * 128
                    sX2r[par].incd(sy.dma_start(out=x2row[:, par * C:(par + 1) * C],
                                                in_=x2_dram[row0:row0 + 128, :]))

                x_load[s] = (sX2r[par], sX2r[par].n)

                @block.vector
                def _(v, s=s, par=par):
                    es_, ev_ = e_load[s]
                    v.wait_ge(es_.h, ev_)
                    xs_, xv_ = x_load[s]
                    v.wait_ge(xs_.h, xv_)
                    if s >= RING:
                        os_, ov_ = o_store[s - RING]
                        v.wait_ge(os_.h, ov_)
                    _li = v.scalar_tensor_tensor(
                        orow[:, par * C:(par + 1) * C],
                        erow[:, par * C:(par + 1) * C],
                        gp_s[:, s:s + 1],
                        x2row[:, par * C:(par + 1) * C],
                        op0=ALU.mult, op1=ALU.add)
                    sDVE.inc(_li)

                comb_dve[s] = sDVE.n

                @block.scalar
                def _(sc, s=s, par=par):
                    sc.wait_ge(sDVE.h, comb_dve[s])
                    row0 = s * 128
                    sOutP[par].incd(sc.dma_start(out=out_d[row0:row0 + 128, :],
                                                 in_=orow[:, par * C:(par + 1) * C]))

                o_store[s] = (sOutP[par], sOutP[par].n)

            @block.sync
            def _(sy):
                for q in range(RING):
                    sOutP[q].wait(sy)

            @block.scalar
            def _(sc):
                for q in range(RING):
                    sOutP[q].wait(sc)

            @block.gpsimd
            def _(g):
                for q in range(RING):
                    sOutP[q].wait(g)

    return nc


def _host_prep(inputs, N=4096, CAP=5120):
    CAPB = CAP // 128
    CHK_T = 1024 if N % 1024 == 0 else N
    x = np.asarray(inputs["L0_attn_x"], np.float32)
    Wq = np.asarray(inputs["L0_attn_Wq"], np.float32)
    Wk = np.asarray(inputs["L0_attn_Wk"], np.float32)
    Wv = np.asarray(inputs["L0_attn_Wv"], np.float32)
    Wo = np.asarray(inputs["L0_attn_Wo"], np.float32)
    bq = np.asarray(inputs["L0_attn_bq"], np.float32)
    bk = np.asarray(inputs["L0_attn_bk"], np.float32)
    bv = np.asarray(inputs["L0_attn_bv"], np.float32)
    bo = np.asarray(inputs["L0_attn_bo"], np.float32)
    ln_g = np.asarray(inputs["L0_attn_ln_g"], np.float32)
    ln_b = np.asarray(inputs["L0_attn_ln_b"], np.float32)
    wg = np.asarray(inputs["L0_ffn_wg"], np.float32)
    w1 = np.asarray(inputs["L0_ffn_w1"], np.float32)
    b1 = np.asarray(inputs["L0_ffn_b1"], np.float32)
    w2 = np.asarray(inputs["L0_ffn_w2"], np.float32)
    b2 = np.asarray(inputs["L0_ffn_b2"], np.float32)

    wqT = np.ascontiguousarray(Wq.T)
    wkT = np.ascontiguousarray(Wk.T)
    wvT = np.ascontiguousarray(Wv.T)
    woT = np.ascontiguousarray(Wo.T)

    consts = np.zeros((128, 256), np.float32)
    consts[:, 0:128] = np.eye(128, dtype=np.float32)
    k_idx = np.arange(128)[:, None]
    i_idx = np.arange(128)[None, :]
    consts[:, 128:256] = (k_idx < i_idx).astype(np.float32)

    rows = np.zeros((3, C), np.float32)
    rows[0] = bq
    rows[1] = bv
    rows[2] = bo

    in_maps = []
    for core in range(NCORES):
        pp = np.zeros((128, PP_COLS), np.float32)
        pp[:, PP_BK:PP_BK + 8] = bk.reshape(8, 128).T
        pp[:, PP_BO:PP_BO + 8] = bo.reshape(8, 128).T
        pp[:, PP_LNG:PP_LNG + 8] = ln_g.reshape(8, 128).T
        pp[:, PP_LNB:PP_LNB + 8] = ln_b.reshape(8, 128).T
        pp[:, PP_WG:PP_WG + 64] = wg.reshape(8, 128, 8).transpose(1, 0, 2).reshape(128, 64)
        pp[:, PP_B1:PP_B1 + 32] = b1[core].reshape(32, 128).T
        pp[:, PP_B2:PP_B2 + 8] = b2[core].reshape(8, 128).T
        pp[0:8, PP_PREF] = (np.arange(8) < core).astype(np.float32)
        pp[:, PP_CBASE] = float(core * N)
        in_maps.append(dict(
            xT=np.ascontiguousarray(x[core].T),
            wqT=wqT, wkT=wkT, wvT=wvT, woT=woT,
            w1e=np.ascontiguousarray(w1[core]),
            w2e=np.ascontiguousarray(w2[core]),
            pp_d=pp, rows_d=rows, consts_d=consts,
            cslotu=(core * CAPB + np.arange(CAPB, dtype=np.uint32))[:, None],
            gidx_d=np.ascontiguousarray(
                ((np.arange(N) // CHK_T) * (NCORES * CHK_T) + core * CHK_T
                 + (np.arange(N) % CHK_T)).astype(np.float32)
                .reshape(N // 128, 128).T),
        ))
    return in_maps


def kernel(**inputs):
    in_maps = _host_prep(inputs)
    nc = build_program()
    res = run_bass_kernel_spmd(nc, in_maps, core_ids=list(range(NCORES)))
    out = np.stack([np.asarray(res.results[i]["out"]) for i in range(NCORES)], axis=0)
    return out.astype(np.float32)
